# version 18
# baseline (speedup 1.0000x reference)
import numpy as np
import concourse.bass as bass
import concourse.mybir as mybir
from concourse.bass_utils import run_bass_kernel_spmd
from contextlib import ExitStack

F32 = mybir.dt.float32
BF16 = mybir.dt.bfloat16
AF = mybir.ActivationFunctionType
ALU = mybir.AluOpType
AX = mybir.AxisListType.X
NP = 128

ENG_CHAN = {"tensor": "cPE", "vector": "cDV", "scalar": "cAC", "gpsimd": "cGP"}
DMA_CHAN = {"sync": "cLD", "gpsimd": "cLW"}


def default_cfg():
    return dict(T=2048, D=1024, NST=16, R=64, HD=64, I=2048, E=8, FF=4096,
                parts=("mamba", "moe1", "attnb", "moe2"))


class Chan:
    def __init__(self, es, nc, name):
        self.h = es.enter_context(nc.semaphore(name))
        self.n = 0


class Buf:
    __slots__ = ("name", "w", "readers")

    def __init__(self, name):
        self.name = name
        self.w = None
        self.readers = []


class Arena:

    def __init__(self, tens, nbytes):
        self.t = tens
        self.nbytes = nbytes
        self.pos = 0
        self.stack = []
        self.hwm = 0
        self.live = []
        self.ghost = []

    def f32(self, cols, bufs=()):
        off = self._take(cols * 4, 4, bufs)
        return self.t[:, off // 4: off // 4 + cols]

    def bf16(self, cols, bufs=()):
        cols2 = (cols + 1) // 2 * 2
        off = self._take(cols2 * 2, 4, bufs)
        return self.t[:, off // 4: off // 4 + cols2 // 2].bitcast(BF16)[:, 0:cols]

    def _take(self, nb, align, bufs):
        if isinstance(bufs, Buf):
            bufs = [bufs]
        bufs = list(bufs)
        p = (self.pos + align - 1) // align * align
        assert p + nb <= self.nbytes, f"SBUF arena overflow: {p + nb} > {self.nbytes}"
        self.pos = p + nb
        self.hwm = max(self.hwm, self.pos)
        for (g0, g1, gbufs) in self.ghost:
            if g0 < p + nb and p < g1:
                for gb in gbufs:
                    ms = ([gb.w] if gb.w else []) + gb.readers
                    for b in bufs:
                        b.readers.extend(ms)
        self.live.append((p, p + nb, bufs))
        return p

    def mark(self):
        self.stack.append(self.pos)

    def release(self):
        newpos = self.stack.pop()
        keep = []
        for rec in self.live:
            (keep if rec[0] < newpos else self.ghost).append(rec)
        self.live = keep
        self.pos = newpos


class MK:
    def __init__(self, cfg):
        self.cfg = cfg
        self.T = cfg["T"]; self.D = cfg["D"]; self.NST = cfg["NST"]; self.R = cfg["R"]
        self.I = cfg["I"]; self.E = cfg["E"]; self.FF = cfg["FF"]; self.HD = cfg["HD"]
        self.DC = self.D // NP
        self.TH = self.T // 2
        self.NH = self.D // self.HD
        self.IC = self.I // NP
        self.FC = self.FF // NP
        self.TQ = min(512, self.TH)
        self.TC = self.T // self.TQ
        self.TCH = self.TH // self.TQ
        self.TT = self.T // NP
        self.TTH = self.TH // NP
        self.nc = bass.Bass("TRN2", target_bir_lowering=False)
        self.es = ExitStack()
        self.plan_idx = 0
        self.pend = {k: [] for k in ("tensor", "vector", "scalar", "sync", "gpsimd")}
        self.waited = {k: {} for k in self.pend}
        self.drained = {k: -1 for k in self.pend}
        self.emitted = {k: -1 for k in self.pend}

    def op(self, eng, reads, writes, fn):
        assert eng in ("tensor", "vector", "scalar", "gpsimd")
        self._queue(eng, ENG_CHAN[eng], reads, writes, fn, inc_amt=1, batch=None)

    def dma(self, queue, reads, writes, fn, batch=None):
        if batch is None:
            batch = ("auto", self.plan_idx)
        self._queue(queue, DMA_CHAN[queue], reads, writes, fn, inc_amt=16, batch=batch)

    def _queue(self, eng, chan_name, reads, writes, fn, inc_amt, batch):
        idx = self.plan_idx
        self.plan_idx += 1
        ch = getattr(self, chan_name)
        waits = []
        need_drain = False

        def consider(ms):
            nonlocal need_drain
            if ms is None:
                return
            mch, mref, msrc, midx = ms
            if msrc == eng and mch == chan_name and inc_amt == 1:
                if midx > self.drained[eng]:
                    need_drain = True
            elif inc_amt == 16 and mch == chan_name:
                pass
            else:
                waits.append((mch, mref))

        for b in reads:
            consider(b.w)
        for b in writes:
            consider(b.w)
            for r in b.readers:
                consider(r)
        if inc_amt == 16:
            if getattr(ch, "cur_batch", None) != batch:
                waits.append((chan_name, [ch.n]))
                ch.cur_batch = batch
                ch.batch_ref = [0]
        if need_drain:
            self.drained[eng] = idx - 1
        ch.n += inc_amt
        if inc_amt == 16:
            ch.batch_ref[0] = ch.n
            ms_new = (chan_name, ch.batch_ref, eng, idx)
        else:
            ms_new = (chan_name, [ch.n], eng, idx)
        for b in writes:
            b.w = ms_new
            b.readers = []
        for b in reads:
            b.readers.append(ms_new)
        chans = {cn: getattr(self, cn) for cn in set([cn for cn, _ in waits] + [chan_name])}

        def emit(e, cache):
            for cn, ref in waits:
                v = ref[0]
                if cache.get(cn, -1) < v:
                    e.wait_ge(chans[cn].h, v)
                    cache[cn] = v
            if need_drain:
                e.drain()
            last = fn(e)
            assert last is not None, f"op fn returned None (plan {idx})"
            if getattr(self, "oplog", None) is not None:
                self.oplog.append((idx, eng, getattr(last, "name", None) if not callable(getattr(last, "name", None)) else last.name(),
                                   [(cn, r[0]) for cn, r in waits],
                                   [b.name for b in reads], [b.name for b in writes]))
            last.then_inc(chans[chan_name].h, inc_amt)
        self.pend[eng].append(emit)

    def flush(self):
        for eng in ("sync", "gpsimd", "vector", "scalar", "tensor"):
            lst = self.pend[eng]
            if not lst:
                continue
            self.pend[eng] = []
            dec = getattr(self.blk, eng)

            def body(e, lst=lst):
                cache = {}
                for f in lst:
                    f(e, cache)
            dec(body)

    def declare_dram(self):
        nc = self.nc
        D, T, R, NST, I, E, FF, DC = self.D, self.T, self.R, self.NST, self.I, self.E, self.FF, self.DC
        dd = {}

        def din(name, shape, dt=F32):
            dd[name] = nc.dram_tensor(name, shape, dt, kind="ExternalInput").ap()

        din("xT", [D, T])
        din("w_inT", [D, 2 * D])
        din("conv_w_r", [NP, DC * 3])
        din("w_xT", [D, NP])
        din("w_dtT", [R, D])
        din("b_dt_r", [NP, DC])
        din("na_r", [NP, DC * NST])
        din("d_param_r", [NP, DC])
        din("w_cT", [NST, D])
        din("w_outT", [D, D])
        din("qkv_w", [D, 3 * D])
        din("qkv_b", [1, 3 * D])
        din("qkv_b_r", [NP, 3 * D // NP])
        din("out_w", [D, D])
        din("out_b_r", [NP, DC])
        din("w1", [D, FF])
        din("b1_r", [NP, FF // NP])
        din("w2", [FF, D])
        din("b2_r", [NP, DC])
        din("g1_r", [NP, DC]); din("be1_r", [NP, DC])
        din("g2_r", [NP, DC]); din("be2_r", [NP, DC])
        din("wg_sT", [D, I]); din("wu_sT", [D, I]); din("wd_sT", [I, D])
        din("wgT", [E, D, I]); din("wuT", [E, D, I]); din("wdT", [E, I, D])
        din("wrT", [D, E])
        din("rb", [1, E])
        din("ident", [NP, NP])
        din("ones_col", [1, NP])
        din("ones_k", [NP, 1])
        din("e8", [E, E * NP])
        din("e2", [2, 2 * NP])
        for name, shape, dt in [("scr_z", [D, T], F32), ("scr_xs", [D, T], F32),
                                ("scr_x", [D, T], F32), ("scr_qf", [D, T], BF16),
                                ("scr_xs_bf", [D, T], BF16), ("scr_b", [NST, T], BF16)]:
            dd[name] = nc.dram_tensor(name, shape, dt, kind="Internal").ap()
        self.out_d = nc.dram_tensor("outT", [D, T], F32, kind="ExternalOutput").ap()
        self.dd = dd
        self.bD = {k: Buf("d_" + k) for k in list(dd.keys()) + ["outT"]}

    def declare_onchip(self):
        nc, es = self.nc, self.es
        D, T, DC = self.D, self.T, self.DC
        cols = 52048 if T > 512 else 26000
        self.sb = es.enter_context(nc.sbuf_tensor("arena", [NP, cols], F32))
        self.A = Arena(self.sb, cols * 4)
        A = self.A
        self.XR = A.f32(DC * T)
        self.XNB = A.bf16(DC * T)
        self.c_identb = A.bf16(NP)
        self.c_identf = A.f32(NP)
        self.c_ones_col = A.f32(NP)
        self.c_ones_colb = A.bf16(NP)
        self.c_ones_k = A.bf16(2)
        self.c_e8 = A.bf16(self.E * NP)
        self.c_conv = A.f32(DC * 3)
        self.c_bdt = A.f32(DC)
        self.c_dpar = A.f32(DC)
        self.c_na = A.f32(DC * self.NST)
        self.c_nb = A.f32(DC * self.NST)
        self.c_g1 = A.f32(DC); self.c_be1 = A.f32(DC)
        self.c_g2 = A.f32(DC); self.c_be2 = A.f32(DC)
        self.c_outb = A.f32(DC)
        self.c_b2 = A.f32(DC)
        self.c_b1 = A.f32(self.FC)
        self.c_rbb = A.bf16(self.E)
        self.c_wrT = A.bf16(DC * self.E)
        self.c_qkvbr = A.f32(3 * D // NP)
        self.c_k = A.f32(4)
        self.ck_map = {1e-5: 0, 1.0: 1, -1.0: 2, -0.5: 3}
        self.rows2 = A.f32(2 * T)
        self.c_e2 = A.f32(2 * NP)

        self.pb = [es.enter_context(nc.psum_tensor(f"pb{i}", [NP, 512], F32)) for i in range(7)]
        self.pbf = es.enter_context(nc.psum_tensor("pbf", [NP, 1024], BF16))
        for nm in ("cLD", "cLW", "cPE", "cDV", "cAC", "cGP"):
            setattr(self, nm, Chan(es, nc, nm))
        self.bXR = [[Buf(f"xr{c}h{h}") for h in range(2)] for c in range(DC)]
        self.bXN = [[Buf(f"xn{c}h{h}") for h in range(2)] for c in range(DC)]
        self.bRows = Buf("rows2")
        self.bConst = Buf("const")
        self.bBank = [Buf(f"pb{i}") for i in range(7)]
        self.bBankF = Buf("pbf")

    def alias_seed(self, dst_bufs, src_bufs):
        for sb in src_bufs:
            ms = ([sb.w] if sb.w else []) + sb.readers
            for b in dst_bufs:
                b.readers.extend(ms)

    def ck(self, v, base=0, cnt=NP):
        i = self.ck_map[v]
        return self.c_k[base:base + cnt, i:i + 1]

    def xr(self, c, t0, t1):
        TH, DC = self.TH, self.DC
        h = t0 // TH
        assert (t1 - 1) // TH == h, (t0, t1)
        base = (h * DC + c) * TH
        return self.XR[:, base + t0 - h * TH: base + t1 - h * TH]

    def xnb(self, c, t0, t1):
        TH, DC = self.TH, self.DC
        h = t0 // TH
        assert (t1 - 1) // TH == h
        base = (h * DC + c) * TH
        return self.XNB[:, base + t0 - h * TH: base + t1 - h * TH]

    def bxr(self, c, t0, t1):
        return [self.bXR[c][h] for h in range(t0 // self.TH, (t1 - 1) // self.TH + 1)]

    def bxn(self, c, t0, t1):
        return [self.bXN[c][h] for h in range(t0 // self.TH, (t1 - 1) // self.TH + 1)]

    def build(self):
        self.declare_dram()
        with self.es:
            self.declare_onchip()
            self.blk = self.es.enter_context(self.nc.Block())
            self.phase_load()
            parts = self.cfg["parts"]
            if "mamba" in parts:
                self.phase_mamba()
            if "moe1" in parts:
                self.phase_moe()
            if "attnb" in parts:
                self.phase_attnb()
            if "moe2" in parts:
                self.phase_moe()
            self.phase_store()
            self.flush()
        return self.nc

    def phase_load(self):
        dd = self.dd
        DC, TH, NST, E = self.DC, self.TH, self.NST, self.E
        for c in range(DC):
            for h in range(2):
                self.dma("sync", [self.bD["xT"]], [self.bXR[c][h]], batch="x0",
                         fn=lambda s, c=c, h=h: s.dma_start(
                             out=self.xr(c, h * TH, (h + 1) * TH),
                             in_=dd["xT"][c * NP:(c + 1) * NP, h * TH:(h + 1) * TH]))
        plain = [(self.c_ones_col[0:1, :], "ones_col"),
                 (self.c_ones_col[32:33, :], "ones_col"),
                 (self.c_ones_col[64:65, :], "ones_col"),
                 (self.c_ones_col[96:97, :], "ones_col"), (self.c_identf[:], "ident"),
                 (self.c_conv[:], "conv_w_r"), (self.c_bdt[:], "b_dt_r"),
                 (self.c_dpar[:], "d_param_r"), (self.c_na[:], "na_r"),
                 (self.c_g1[:], "g1_r"), (self.c_be1[:], "be1_r"),
                 (self.c_g2[:], "g2_r"), (self.c_be2[:], "be2_r"),
                 (self.c_outb[:], "out_b_r"), (self.c_b2[:], "b2_r"),
                 (self.c_b1[:], "b1_r"), (self.c_qkvbr[:], "qkv_b_r"),
                 (self.c_e2[0:2, :], "e2")]
        for dst, nm in plain:
            self.dma("sync", [self.bD[nm]], [self.bConst],
                     lambda s, dst=dst, nm=nm: s.dma_start(out=dst, in_=dd[nm]))
        casts = [(self.c_identb[:], "ident"), (self.c_ones_colb[0:1, :], "ones_col"),
                 (self.c_ones_k[:, 0:1], "ones_k"),
                 (self.c_e8[0:E, :], "e8"), (self.c_rbb[0:1, :], "rb")]
        for dst, nm in casts:
            self.dma("gpsimd", [self.bD[nm]], [self.bConst],
                     lambda g, dst=dst, nm=nm: g.dma_start(out=dst, in_=dd[nm]))
        self.dma("gpsimd", [self.bD["wrT"]], [self.bConst],
                 lambda g: g.dma_start(
                     out=self.c_wrT.rearrange("p (c e) -> p c e", e=self.E),
                     in_=dd["wrT"].rearrange("(c p) e -> p c e", p=NP)))
        for v, i in self.ck_map.items():
            self.op("vector", [], [self.bConst],
                    lambda vv, v=v, i=i: vv.memset(self.c_k[:, i:i + 1], v))
        self.op("scalar", [self.bConst], [self.bConst],
                lambda a: a.activation(out=self.c_na[:], in_=self.c_na[:], func=AF.Exp))
        self.op("vector", [self.bConst], [self.bConst],
                lambda v: v.tensor_scalar_mul(out=self.c_na[:], in0=self.c_na[:], scalar1=-1.0))
        self.op("vector", [self.bConst], [self.bConst],
                lambda v: v.tensor_scalar_mul(out=self.c_nb[:], in0=self.c_na[:], scalar1=1e-5))

    def emit_ln_rows(self, t0g, t1g, *, fused_attnb=False, eps=1e-5):
        DC, TQ, T = self.DC, self.TQ, self.T
        W = t1g - t0g
        ntile = W // NP
        NRC = 12
        self.A.mark()
        bxq, bST, bRM = Buf("ln_xq"), Buf("ln_st"), Buf("ln_rm")
        h0 = t0g // self.TH
        nhalf = (t1g - t0g + self.TH - 1) // self.TH
        xq = self.A.bf16(DC * W, bxq)
        stm = self.A.f32(2 * ntile, bST)
        rmm = self.A.f32(NRC * ntile, bRM)

        def xb_v(c, a, b):
            return self.xnb(c, t0g + a, t0g + b)
        for c in range(DC):
            for hh in range(nhalf):
                w0 = hh * self.TH
                w1 = min(W, (hh + 1) * self.TH)
                self.op("vector", self.bxr(c, t0g + w0, t0g + w1), [self.bXN[c][h0 + hh]],
                        lambda v, c=c, w0=w0, w1=w1: v.tensor_copy(
                            out=self.xnb(c, t0g + w0, t0g + w1),
                            in_=self.xr(c, t0g + w0, t0g + w1)))
            self.op("scalar", self.bxr(c, t0g, t1g), [bxq],
                    lambda a, c=c: a.activation(out=xq[:, c * W:(c + 1) * W],
                                                in_=self.xr(c, t0g, t1g), func=AF.Square))
        for j in range(ntile):
            kb = j % 2

            def mm(t, j=j, kb=kb):
                for c in range(DC):
                    t.matmul(out=self.pb[kb][:, 0:1],
                             lhsT=self.xnb(c, t0g + j * NP, t0g + (j + 1) * NP),
                             rhs=self.c_ones_k[:, 0:1],
                             start=(c == 0), stop=False)
                for c in range(DC):
                    ii = t.matmul(out=self.pb[kb][:, 1:2],
                                  lhsT=xq[:, c * W + j * NP:c * W + (j + 1) * NP],
                                  rhs=self.c_ones_k[:, 0:1],
                                  start=False, stop=(c == DC - 1))
                return ii
            xbb = [self.bXN[c][h0 + (j * NP) // self.TH] for c in range(DC)]
            self.op("tensor", xbb + [bxq, self.bConst], [self.bBank[kb]], mm)
            self.op("vector", [self.bBank[kb]], [bST],
                    lambda v, j=j, kb=kb: v.tensor_copy(out=stm[:, 2 * j:2 * j + 2],
                                                        in_=self.pb[kb][:, 0:2]))
        D = self.D

        def sl(k):
            return rmm.rearrange("p (n k) -> p n k", k=NRC)[:, :, k]
        s_mu, s_ms, s_var, s_tmp, s_a, s_b = (sl(k) for k in range(6))
        self.op("vector", [bST], [bRM],
                lambda v: v.tensor_scalar_mul(
                    out=s_mu, in0=stm.rearrange("p (n k) -> p n k", k=2)[:, :, 0],
                    scalar1=1.0 / D))
        self.op("vector", [bST], [bRM],
                lambda v: v.tensor_scalar_mul(
                    out=s_ms, in0=stm.rearrange("p (n k) -> p n k", k=2)[:, :, 1],
                    scalar1=1.0 / D))
        self.op("vector", [bRM], [bRM],
                lambda v: v.tensor_tensor(out=s_tmp, in0=s_mu, in1=s_mu, op=ALU.mult))
        self.op("vector", [bRM], [bRM],
                lambda v: v.tensor_tensor(out=s_var, in0=s_ms, in1=s_tmp, op=ALU.subtract))
        self.op("scalar", [bRM, self.bConst], [bRM],
                lambda a: a.activation(out=s_tmp, in_=s_var, func=AF.Ln, bias=self.ck(eps)))
        self.op("scalar", [bRM, self.bConst], [bRM],
                lambda a: a.activation(out=s_a, in_=s_tmp, func=AF.Exp, scale=self.ck(-0.5)))
        self.op("vector", [bRM], [bRM],
                lambda v: v.scalar_tensor_tensor(out=s_b, in0=s_mu, scalar=-1.0,
                                                 in1=s_a, op0=ALU.mult, op1=ALU.mult))
        npairs = 1
        if fused_attnb:
            s_a2, s_b2, s_s1 = sl(6), sl(7), sl(8)
            self.op("vector", [bRM], [bRM],
                    lambda v: v.tensor_tensor(out=s_s1, in0=s_var, in1=s_a, op=ALU.mult))
            self.op("vector", [bRM], [bRM],
                    lambda v: v.tensor_tensor(out=s_s1, in0=s_s1, in1=s_a, op=ALU.mult))
            self.op("scalar", [bRM, self.bConst], [bRM],
                    lambda a: a.activation(out=s_s1, in_=s_s1, func=AF.Ln, bias=self.ck(1e-5)))
            self.op("scalar", [bRM, self.bConst], [bRM],
                    lambda a: a.activation(out=s_s1, in_=s_s1, func=AF.Exp, scale=self.ck(-0.5)))
            self.op("vector", [bRM], [bRM],
                    lambda v: v.tensor_tensor(out=s_a2, in0=s_a, in1=s_s1, op=ALU.mult))
            self.op("vector", [bRM], [bRM],
                    lambda v: v.tensor_tensor(out=s_b2, in0=s_b, in1=s_s1, op=ALU.mult))
            npairs = 2
        for p_ in range(npairs):
            base = 4 if p_ == 0 else 6
            coff = 0 if p_ == 0 else T
            for j0 in range(0, ntile, 4):
                jn = min(4, ntile - j0)

                def trp(t, j0=j0, jn=jn, base=base):
                    for u in range(jn):
                        ii = t.transpose(
                            out=self.pb[2][0:2, u * NP:(u + 1) * NP],
                            in_=rmm[:, (j0 + u) * NRC + base:(j0 + u) * NRC + base + 2],
                            identity=self.c_identf[:])
                    return ii
                self.op("tensor", [bRM, self.bConst], [self.bBank[2]], trp)
                self.op("scalar", [self.bBank[2]], [self.bRows],
                        lambda a, j0=j0, jn=jn, coff=coff: a.copy(
                            out=self.rows2[0:2, coff + t0g + j0 * NP:coff + t0g + (j0 + jn) * NP],
                            in_=self.pb[2][0:2, 0:jn * NP]))
        self.A.release()

    def emit_apply_rows(self, coff, t0g, t1g, src_fn, src_bufs_fn, dst_fn, dst_bufs_fn,
                        tmp, btmp, gcol=None, becol=None, batched=None):
        DC, TQ, TH = self.DC, self.TQ, self.TH
        ntile = (t1g - t0g) // TQ
        for i in range(ntile):
            t0, t1 = t0g + i * TQ, t0g + (i + 1) * TQ
            h = t0 // TH
            lt0, lt1 = t0 - h * TH, t1 - h * TH
            kA, kB = (4, 5) if i % 2 == 0 else (6, 3)

            def mm(t, t0=t0, t1=t1, kA=kA, kB=kB, coff=coff):
                t.matmul(out=self.pb[kA][:, 0:TQ], lhsT=self.c_e2[0:2, 0:NP],
                         rhs=self.rows2[0:2, coff + t0:coff + t1], start=True, stop=True)
                return t.matmul(out=self.pb[kB][:, 0:TQ], lhsT=self.c_e2[0:2, NP:2 * NP],
                                rhs=self.rows2[0:2, coff + t0:coff + t1], start=True, stop=True)
            self.op("tensor", [self.bRows, self.bConst], [self.bBank[kA], self.bBank[kB]], mm)
            srcb = [b for cc in range(DC) for b in src_bufs_fn(cc, t0, t1)]
            dstb = [b for cc in range(DC) for b in dst_bufs_fn(cc, t0, t1)]
            tmp3 = tmp.rearrange("p (c t) -> p c t", t=TQ)
            bcA = self.pb[kA][:, 0:TQ].unsqueeze(1).broadcast_to([NP, DC, TQ])
            bcB = self.pb[kB][:, 0:TQ].unsqueeze(1).broadcast_to([NP, DC, TQ])
            if batched is not None:
                src3 = batched[0](h, lt0, lt1)
                dst3 = batched[1](h, lt0, lt1) if gcol is None else tmp3
                self.op("vector", srcb + [self.bBank[kA]], [btmp],
                        lambda v, src3=src3, bcA=bcA: v.tensor_tensor(
                            out=tmp3, in0=src3, in1=bcA, op=ALU.mult))
                self.op("vector", [btmp, self.bBank[kB]],
                        (dstb if gcol is None else [btmp]),
                        lambda v, dst3=dst3, bcB=bcB: v.tensor_tensor(
                            out=dst3, in0=tmp3, in1=bcB, op=ALU.add))
                if gcol is not None:
                    for c in range(DC):
                        self.op("vector", [btmp, self.bConst], dst_bufs_fn(c, t0, t1),
                                lambda v, c=c, t0=t0, t1=t1: v.tensor_scalar(
                                    out=dst_fn(c, t0, t1),
                                    in0=tmp[:, c * TQ:(c + 1) * TQ],
                                    scalar1=gcol[:, c:c + 1], scalar2=becol[:, c:c + 1],
                                    op0=ALU.mult, op1=ALU.add))
                continue
            for c in range(DC):
                self.op("vector", src_bufs_fn(c, t0, t1) + [self.bBank[kA]], [btmp],
                        lambda v, c=c, t0=t0, t1=t1, kA=kA: v.tensor_tensor(
                            out=tmp[:, (c % 2) * TQ:(c % 2) * TQ + TQ],
                            in0=src_fn(c, t0, t1), in1=self.pb[kA][:, 0:TQ], op=ALU.mult))
                if gcol is None:
                    self.op("vector", [btmp, self.bBank[kB]], dst_bufs_fn(c, t0, t1),
                            lambda v, c=c, t0=t0, t1=t1, kB=kB: v.tensor_tensor(
                                out=dst_fn(c, t0, t1),
                                in0=tmp[:, (c % 2) * TQ:(c % 2) * TQ + TQ],
                                in1=self.pb[kB][:, 0:TQ], op=ALU.add))
                else:
                    self.op("vector", [btmp, self.bBank[kB]], [btmp],
                            lambda v, c=c, kB=kB: v.tensor_tensor(
                                out=tmp[:, (c % 2) * TQ:(c % 2) * TQ + TQ],
                                in0=tmp[:, (c % 2) * TQ:(c % 2) * TQ + TQ],
                                in1=self.pb[kB][:, 0:TQ], op=ALU.add))
                    self.op("vector", [btmp, self.bConst], dst_bufs_fn(c, t0, t1),
                            lambda v, c=c, t0=t0, t1=t1: v.tensor_scalar(
                                out=dst_fn(c, t0, t1),
                                in0=tmp[:, (c % 2) * TQ:(c % 2) * TQ + TQ],
                                scalar1=gcol[:, c:c + 1], scalar2=becol[:, c:c + 1],
                                op0=ALU.mult, op1=ALU.add))

    def ln_into_xnb(self, t0g, t1g, eps=1e-5):
        self.emit_ln_rows(t0g, t1g, eps=eps)
        self.A.mark()
        btmp = Buf("ln_tmp")
        tmp = self.A.f32(self.DC * self.TQ, btmp)
        self.emit_apply_rows(0, t0g, t1g, self.xr, self.bxr, self.xnb,
                             lambda c, a, b: self.bxn(c, a, b), tmp, btmp,
                             batched=(self.xr3, self.xnb3))
        self.A.release()

    def phase_mamba(self):
        dd = self.dd
        D, T, DC, TQ, TC, NST, R, TH = self.D, self.T, self.DC, self.TQ, self.TC, self.NST, self.R, self.TH
        self.ln_into_xnb(0, TH)
        self.ln_into_xnb(TH, T)

        for c in range(DC):
            for h in range(2):
                self.dma("sync", [self.bXR[c][h]], [self.bD["scr_x"]], batch="mamxsp",
                         fn=lambda s, c=c, h=h: s.dma_start(
                             out=dd["scr_x"][c * NP:(c + 1) * NP, h * TH:(h + 1) * TH],
                             in_=self.xr(c, h * TH, (h + 1) * TH)))
        self.A.mark()
        SPL = NP
        bWx, bXPS, bWdt, bWc = (Buf(x) for x in ("w_x", "xps", "w_dt", "w_c"))
        bDtp = bBp = bCp = bXPS
        bT1, bT2 = Buf("t1"), Buf("t2")
        t1b = self.A.f32(T, bT1)
        t2b = self.A.f32(T, bT2)
        xps = self.A.bf16(T, bXPS)
        dtp_bf = xps
        bp_bf = xps
        cp_bf = xps
        w_x_buf = self.A.bf16(DC * SPL, bWx)
        w_dt_buf = self.A.bf16(DC * NP, bWdt)
        w_c_buf = self.A.bf16(DC * NP, bWc)
        for cin in range(DC):
            self.dma("gpsimd", [self.bD["w_xT"]], [bWx], batch="wx",
                     fn=lambda g, cin=cin: g.dma_start(
                         out=w_x_buf[:, cin * SPL:(cin + 1) * SPL],
                         in_=dd["w_xT"][cin * NP:(cin + 1) * NP, :]))
        self.dma("gpsimd", [self.bD["w_dtT"]], [bWdt], batch="wx",
                 fn=lambda g: g.dma_start(out=w_dt_buf[64:64 + R, :], in_=dd["w_dtT"]))
        self.dma("gpsimd", [self.bD["w_cT"]], [bWc], batch="wx",
                 fn=lambda g: g.dma_start(out=w_c_buf[32:32 + NST, :], in_=dd["w_cT"]))

        self.A.mark()
        OCH = min(4, 2 * DC)
        n_outg = (2 * DC) // OCH
        bWin = [Buf(f"w_in{c}") for c in range(DC)]
        bXC2 = [Buf("xc0"), Buf("xc1")]
        bXSs = [Buf("xs_st0"), Buf("xs_st1")]
        bZs = [Buf("z_st0"), Buf("z_st1")]
        w_in_buf = self.A.bf16(DC * OCH * NP, bWin)
        xc_buf2 = [self.A.f32(T, bXC2[0]), self.A.f32(T, bXC2[1])]
        xs_stage = [self.A.f32(T, bXSs[0]), self.A.f32(T, bXSs[1])]
        z_stage = [self.A.f32(TQ, bZs[0]), self.A.f32(TQ, bZs[1])]
        OW = OCH * NP
        for gidx in range(n_outg):
            for cin in range(DC):
                self.dma("gpsimd", [self.bD["w_inT"]], [bWin[cin]], batch=("w_in", gidx),
                         fn=lambda g, cin=cin, gidx=gidx: g.dma_start(
                             out=w_in_buf[:, cin * OW:(cin + 1) * OW],
                             in_=dd["w_inT"][cin * NP:(cin + 1) * NP, gidx * OW:(gidx + 1) * OW]))
            for oc in range(OCH):
                cout = gidx * OCH + oc
                is_z = cout >= DC
                for i in range(TC):
                    t0, t1 = i * TQ, (i + 1) * TQ
                    bk = (oc * TC + i) % 3

                    def mm(t, oc=oc, t0=t0, t1=t1, bk=bk):
                        for cin in range(DC):
                            ii = t.matmul(out=self.pb[bk][:, 0:TQ],
                                          lhsT=w_in_buf[:, cin * OW + oc * NP:cin * OW + (oc + 1) * NP],
                                          rhs=self.xnb(cin, t0, t1),
                                          start=(cin == 0), stop=(cin == DC - 1))
                        return ii
                    rbufs = list(bWin) + [b for cc in range(DC) for b in self.bxn(cc, t0, t1)]
                    self.op("tensor", rbufs, [self.bBank[bk]], mm)
                    if not is_z:
                        self.op("vector", [self.bBank[bk]], [bXC2[cout % 2]],
                                lambda v, t0=t0, t1=t1, bk=bk, cout=cout: v.tensor_copy(
                                    out=xc_buf2[cout % 2][:, t0:t1], in_=self.pb[bk][:, 0:TQ]))
                    else:
                        zc = cout - DC
                        st = i % 2
                        self.op("scalar", [self.bBank[bk]], [bZs[st]],
                                lambda a, bk=bk, st=st: a.copy(out=z_stage[st][:, 0:TQ],
                                                               in_=self.pb[bk][:, 0:TQ]))
                        self.dma("sync", [bZs[st]], [self.bD["scr_z"]],
                                 fn=lambda s, zc=zc, i=i, st=st: s.dma_start(
                                     out=dd["scr_z"][zc * NP:(zc + 1) * NP, i * TQ:(i + 1) * TQ],
                                     in_=z_stage[st][:, 0:TQ]))
                if not is_z:
                    c = cout
                    stg = xs_stage[c % 2]
                    bst = bXSs[c % 2]
                    xcb = xc_buf2[c % 2]
                    bxc = bXC2[c % 2]
                    self.op("vector", [bxc, self.bConst], [bst],
                            lambda v, c=c, stg=stg, xcb=xcb: v.tensor_scalar_mul(
                                out=stg[:], in0=xcb[:],
                                scalar1=self.c_conv[:, c * 3 + 1:c * 3 + 2]))
                    self.op("vector", [bxc, self.bConst, bst], [bst],
                            lambda v, c=c, stg=stg, xcb=xcb: v.scalar_tensor_tensor(
                                out=stg[:, 1:T], in0=xcb[:, 0:T - 1],
                                scalar=self.c_conv[:, c * 3 + 0:c * 3 + 1],
                                in1=stg[:, 1:T], op0=ALU.mult, op1=ALU.add))
                    self.op("vector", [bxc, self.bConst, bst], [bst],
                            lambda v, c=c, stg=stg, xcb=xcb: v.scalar_tensor_tensor(
                                out=stg[:, 0:T - 1], in0=xcb[:, 1:T],
                                scalar=self.c_conv[:, c * 3 + 2:c * 3 + 3],
                                in1=stg[:, 0:T - 1], op0=ALU.mult, op1=ALU.add))
                    self.op("scalar", [bst], [bst],
                            lambda a, stg=stg: a.activation(out=stg[:], in_=stg[:], func=AF.Silu))
                    self.dma("sync", [bst], [self.bD["scr_xs"]],
                             fn=lambda s, c=c, stg=stg: s.dma_start(
                                 out=dd["scr_xs"][c * NP:(c + 1) * NP, :], in_=stg[:]))
                    self.dma("gpsimd", [bst], [self.bD["scr_xs_bf"]],
                             fn=lambda g, c=c, stg=stg: g.dma_start(
                                 out=dd["scr_xs_bf"][c * NP:(c + 1) * NP, :], in_=stg[:]))
        self.A.release()

        self.A.mark()
        bXBT = [Buf("xbt0"), Buf("xbt1")]
        xbt = [self.A.bf16(T, bXBT[0]), self.A.bf16(T, bXBT[1])]
        bkA = list(range(TC))
        for cin in range(DC):
            pp = cin % 2
            self.dma("sync", [self.bD["scr_xs_bf"]], [bXBT[pp]],
                     fn=lambda s, cin=cin, pp=pp: s.dma_start(
                         out=xbt[pp][:], in_=dd["scr_xs_bf"][cin * NP:(cin + 1) * NP, :]))

            def mmx(t, cin=cin, pp=pp):
                for i in range(TC):
                    t0, t1 = i * TQ, (i + 1) * TQ
                    ii = t.matmul(out=self.pb[bkA[i]][0:SPL, 0:TQ],
                                  lhsT=w_x_buf[:, cin * SPL:(cin + 1) * SPL],
                                  rhs=xbt[pp][:, t0:t1],
                                  start=(cin == 0), stop=(cin == DC - 1))
                return ii
            self.op("tensor", [bWx, bXBT[pp]],
                    [self.bBank[k] for k in bkA], mmx)
        for i in range(TC):
            t0, t1 = i * TQ, (i + 1) * TQ

            self.op("vector", [self.bBank[bkA[i]]], [bXPS],
                    lambda v, i=i, t0=t0, t1=t1: v.tensor_copy(
                        out=xps[:, t0:t1], in_=self.pb[bkA[i]][:, 0:TQ]))
        self.dma("sync", [bXPS], [self.bD["scr_b"]], batch="scrb",
                 fn=lambda s: s.dma_start(out=dd["scr_b"], in_=xps[0:NST, 0:T]))
        self.A.release()

        self.A.mark()
        _bdt = Buf("dt0")
        bDt = [_bdt, _bdt]
        bU, bXsl = Buf("u"), Buf("xsl")
        bS2 = [Buf("s2_0"), Buf("s2_1")]
        bBcbH = [[Buf("bcb00"), Buf("bcb01")], [Buf("bcb10"), Buf("bcb11")]]
        bA2 = [Buf("A2_0"), Buf("A2_1")]
        bBx2 = [Buf("Bx2_0"), Buf("Bx2_1")]
        bcb = [self.A.bf16(2 * T, bBcbH[0]), self.A.bf16(2 * T, bBcbH[1])]
        _dtb = self.A.f32(T, bDt[0])
        dtb2 = [_dtb, _dtb]
        ub = self.A.f32(T, bU)
        xsl = self.A.f32(T, bXsl)
        s2a = self.A.f32(2 * T, bS2[0])
        t1_off = t1b.offset
        t2_off = t2b.offset
        assert t2_off == t1_off + T, (t1_off, t2_off)
        s2b = self.sb[:, t1_off:t1_off + 2 * T]
        self.alias_seed([bS2[1]], [bT1, bT2])
        s22 = [s2a, s2b]
        allxrb = [self.bXR[cc][hh] for cc in range(DC) for hh in range(2)]
        if DC * T >= 8 * T:
            A2 = [self.XR[:, 0:2 * T], self.XR[:, 2 * T:4 * T]]
            Bx2 = [self.XR[:, 4 * T:6 * T], self.XR[:, 6 * T:8 * T]]
            self.alias_seed(bA2 + bBx2, allxrb)
            scan_overlay = True
        else:
            A2 = [self.A.f32(2 * T, bA2[0]), self.A.f32(2 * T, bA2[1])]
            Bx2 = [self.A.f32(2 * T, bBx2[0]), self.A.f32(2 * T, bBx2[1])]
            scan_overlay = False

        def ybf_v(c, a, b):
            return self.XNB[:, c * T + a:c * T + b]
        bYBF = Buf("ybf")
        allxn = [b for cc in range(DC) for hh in range(2) for b in [self.bXN[cc][hh]]]
        NPAIR = NST // 2

        for c in range(DC):
            dtb, bDtc = dtb2[c % 2], bDt[c % 2]
            self.dma("sync", [self.bD["scr_xs"], bU, bT1], [bXsl],
                     fn=lambda s, c=c: s.dma_start(out=xsl[:],
                                                   in_=dd["scr_xs"][c * NP:(c + 1) * NP, :]))
            for i in range(TC):
                t0, t1 = i * TQ, (i + 1) * TQ
                self.op("tensor", [bWdt, bXPS], [self.bBank[6]],
                        lambda t, c=c, t0=t0, t1=t1: t.matmul(
                            out=self.pb[6][:, 0:TQ],
                            lhsT=w_dt_buf[64:64 + R, c * NP:(c + 1) * NP],
                            rhs=xps[64:64 + R, t0:t1], start=True, stop=True))
                self.op("vector", [self.bBank[6], self.bConst], [bDtc],
                        lambda v, c=c, t0=t0, t1=t1, dtb=dtb: v.tensor_scalar(
                            out=dtb[:, t0:t1], in0=self.pb[6][:, 0:TQ],
                            scalar1=self.c_bdt[:, c:c + 1], scalar2=None, op0=ALU.add))
            self.op("vector", [bDtc], [bT1],
                    lambda v, dtb=dtb: v.scalar_tensor_tensor(
                        out=t1b[:], in0=dtb[:], scalar=-1.0,
                        in1=dtb[:], op0=ALU.mult, op1=ALU.max))
            self.op("scalar", [bT1, self.bConst], [bT2],
                    lambda a: a.activation(out=t2b[:], in_=t1b[:], func=AF.Exp, scale=self.ck(-1.0)))
            self.op("scalar", [bT2, self.bConst], [bT1],
                    lambda a: a.activation(out=t1b[:], in_=t2b[:], func=AF.Ln, bias=self.ck(1.0)))
            self.op("vector", [bT1, bDtc], [bDtc],
                    lambda v, dtb=dtb: v.scalar_tensor_tensor(
                        out=dtb[:], in0=dtb[:], scalar=0.0,
                        in1=t1b[:], op0=ALU.max, op1=ALU.add))
            self.op("vector", [bDtc, bXsl], [bU],
                    lambda v, dtb=dtb: v.scalar_tensor_tensor(
                        out=ub[:], in0=dtb[:], scalar=1e-5,
                        in1=xsl[:], op0=ALU.add, op1=ALU.mult))
            self.op("vector", [bDtc], [bDtc],
                    lambda v, dtb=dtb: v.memset(dtb[:, 0:1], 1e30))
            def prep(j, c=c, dtb=dtb, bDtc=bDtc):
                pj = j % 2
                Ab, Bxb, bAb, bBxb = A2[pj], Bx2[pj], bA2[pj], bBx2[pj]
                bce = bcb[pj]
                for half_n in range(2):
                    n = 2 * j + half_n
                    self.op("scalar", [bDtc, self.bConst], [bAb],
                            lambda a, c=c, n=n, half_n=half_n, Ab=Ab, dtb=dtb: a.activation(
                                out=Ab[:, half_n * T:(half_n + 1) * T], in_=dtb[:], func=AF.Exp,
                                scale=self.c_na[:, c * NST + n:c * NST + n + 1],
                                bias=self.c_nb[:, c * NST + n:c * NST + n + 1]))
                    self.dma("sync", [self.bD["scr_b"]], [bBcbH[pj][half_n]], batch=("bbc", c, j),
                             fn=lambda s, n=n, half_n=half_n, bce=bce: s.dma_start(
                                 out=bce[:, half_n * T:(half_n + 1) * T],
                                 in_=dd["scr_b"][n:n + 1, :].broadcast_to([NP, T])))
                    self.op("vector", [bBcbH[pj][half_n], bU], [bBxb],
                            lambda v, half_n=half_n, bce=bce, Bxb=Bxb:
                            v.tensor_tensor(
                                out=Bxb[:, half_n * T:(half_n + 1) * T],
                                in0=ub[:],
                                in1=bce[:, half_n * T:(half_n + 1) * T], op=ALU.mult))

            prep(0)
            for j in range(NPAIR):
                pj = j % 2
                Ab, Bxb, bAb, bBxb = A2[pj], Bx2[pj], bA2[pj], bBx2[pj]
                s2, bS = s22[pj], bS2[pj]
                alias_tt = [bT1, bT2] if pj == 1 else []
                self.op("vector", [bAb, bBxb], [bS] + alias_tt,
                        lambda v, Ab=Ab, Bxb=Bxb, s2=s2: v.tensor_tensor_scan(
                            out=s2[:], data0=Ab[:], data1=Bxb[:],
                            initial=0.0, op0=ALU.mult, op1=ALU.add))
                if j + 1 < NPAIR:
                    prep(j + 1)

                def accmm(t, j=j, s2=s2):
                    for hn in range(2):
                        for i in range(TC):
                            ii = t.matmul(out=self.pb[i][:, 0:TQ], lhsT=self.c_identf[:],
                                          rhs=s2[:, hn * T + i * TQ:hn * T + (i + 1) * TQ],
                                          start=(j == 0 and hn == 0),
                                          stop=(j == NPAIR - 1 and hn == 1))
                    return ii
                self.op("tensor", [bS, self.bConst] + alias_tt,
                        [self.bBank[i] for i in range(TC)], accmm)
            for i in range(TC):
                t0, t1 = i * TQ, (i + 1) * TQ
                self.op("tensor", [bWc, bXPS], [self.bBank[6]],
                        lambda t, c=c, t0=t0, t1=t1: t.matmul(
                            out=self.pb[6][:, 0:TQ],
                            lhsT=w_c_buf[32:32 + NST, c * NP:(c + 1) * NP],
                            rhs=xps[32:32 + NST, t0:t1], start=True, stop=True))
                self.op("scalar", [self.bBank[6]], [bT2],
                        lambda a, t0=t0, t1=t1: a.copy(out=t2b[:, t0:t1],
                                                       in_=self.pb[6][:, 0:TQ]))
                self.op("vector", [bT2, self.bBank[i]], [bT1],
                        lambda v, i=i, t0=t0, t1=t1: v.tensor_tensor(
                            out=t1b[:, t0:t1], in0=t2b[:, t0:t1],
                            in1=self.pb[i][:, 0:TQ], op=ALU.mult))
            self.op("vector", [bT1, bXsl, self.bConst], [bYBF] + (allxn if c == 0 else []),
                    lambda v, c=c: v.scalar_tensor_tensor(
                        out=ybf_v(c, 0, T), in0=xsl[:],
                        scalar=self.c_dpar[:, c:c + 1], in1=t1b[:],
                        op0=ALU.mult, op1=ALU.add))
        self.A.release()

        for c in range(DC):
            for h in range(2):
                self.dma("sync", [self.bD["scr_x"]],
                         [self.bXR[c][h]] + (bA2 + bBx2 if (scan_overlay and c == 0 and h == 0) else []),
                         batch="mamxrl",
                         fn=lambda s, c=c, h=h: s.dma_start(
                             out=self.xr(c, h * TH, (h + 1) * TH),
                             in_=dd["scr_x"][c * NP:(c + 1) * NP, h * TH:(h + 1) * TH]))

        self.A.mark()
        bWo = [Buf(f"w_out{c}") for c in range(DC)]
        bZl = [Buf("zld0"), Buf("zld1")]
        bSzt = [Buf("szt0"), Buf("szt1")]
        bWT = Buf("wtmp")
        wtmp = self.A.f32(T, bWT)
        w_out_buf = self.A.bf16(DC * DC * NP, bWo)
        zld = [self.A.f32(TQ, bZl[0]), self.A.f32(TQ, bZl[1])]
        szt = [self.A.f32(TQ, bSzt[0]), self.A.f32(TQ, bSzt[1])]
        for cin in range(DC):
            self.dma("gpsimd", [self.bD["w_outT"]], [bWo[cin]], batch="w_out",
                     fn=lambda g, cin=cin: g.dma_start(
                         out=w_out_buf[:, cin * DC * NP:(cin + 1) * DC * NP],
                         in_=dd["w_outT"][cin * NP:(cin + 1) * NP, :]))
        for cout in range(DC):
            for i in range(TC):
                t0, t1 = i * TQ, (i + 1) * TQ
                st = i % 2
                kb = 4 + (i % 2)
                self.dma("sync", [self.bD["scr_z"]], [bZl[st]],
                         fn=lambda s, cout=cout, t0=t0, t1=t1, st=st: s.dma_start(
                             out=zld[st][:, 0:TQ],
                             in_=dd["scr_z"][cout * NP:(cout + 1) * NP, t0:t1]))
                self.op("scalar", [bZl[st]], [bSzt[st]],
                        lambda a, st=st: a.activation(out=szt[st][:, 0:TQ], in_=zld[st][:, 0:TQ],
                                                      func=AF.Silu))

                def mm(t, cout=cout, t0=t0, t1=t1, kb=kb):
                    for cin in range(DC):
                        ii = t.matmul(out=self.pb[kb][:, 0:TQ],
                                      lhsT=w_out_buf[:, cin * DC * NP + cout * NP:
                                                     cin * DC * NP + (cout + 1) * NP],
                                      rhs=ybf_v(cin, t0, t1),
                                      start=(cin == 0), stop=(cin == DC - 1))
                    return ii
                self.op("tensor", list(bWo) + [bYBF], [self.bBank[kb]], mm)
                self.op("vector", [self.bBank[kb], bSzt[st]], [bWT],
                        lambda v, t0=t0, t1=t1, kb=kb, st=st: v.tensor_tensor(
                            out=wtmp[:, t0:t1], in0=szt[st][:, 0:TQ],
                            in1=self.pb[kb][:, 0:TQ], op=ALU.mult))
                self.op("vector", [bWT] + self.bxr(cout, t0, t1), self.bxr(cout, t0, t1),
                        lambda v, cout=cout, t0=t0, t1=t1: v.tensor_tensor(
                            out=self.xr(cout, t0, t1), in0=self.xr(cout, t0, t1),
                            in1=wtmp[:, t0:t1], op=ALU.add))
        self.A.release()
        self.alias_seed([b for cc in range(DC) for hh in range(2)
                         for b in [self.bXN[cc][hh]]], [bYBF])
        self.A.release()

    def phase_moe(self):
        dd = self.dd
        D, T, DC, TQ, NST, TH, I, IC, E = self.D, self.T, self.DC, self.TQ, self.NST, self.TH, self.I, self.IC, self.E
        for h in range(2):
            t0h, t1h = h * TH, (h + 1) * TH
            oh = 1 - h
            for c in range(DC):
                self.dma("sync", [self.bXR[c][oh]], [self.bD["scr_x"]], batch=("mxsp", h),
                         fn=lambda s, c=c, oh=oh: s.dma_start(
                             out=dd["scr_x"][c * NP:(c + 1) * NP, oh * TH:(oh + 1) * TH],
                             in_=self.xr(c, oh * TH, (oh + 1) * TH)))
            self.A.mark()
            NQW0 = 4 if IC % 4 == 0 else 1
            IQ0 = I // NQW0
            bWgu = [[Buf(f"wguq{p}_{c}") for c in range(2 * DC)] for p in range(2)]
            wguq = [self.A.bf16(2 * DC * IQ0, bWgu[0]), self.A.bf16(2 * DC * IQ0, bWgu[1])]
            self.ln_into_xnb(t0h, t1h)
            bCWR, bL8, bMR, bL2, bMS, bSG, bCW = (Buf(x) for x in
                                                  ("cwr", "l8", "mr", "l2", "msk", "sg", "cwb"))
            CWR = self.A.bf16(TH, bCWR)
            l8 = self.A.f32(8, bL8)
            mr8 = self.A.f32(8, bMR)
            l2_ = self.A.f32(8, bL2)
            msk = self.A.f32(8, bMS)
            sg8 = self.A.f32(8, bSG)
            cwb_ = self.A.bf16(8, bCW)
            for j in range(self.TTH):
                jt0 = t0h + j * NP

                def mmr(t, jt0=jt0):
                    for cin in range(DC):
                        t.matmul(out=self.pb[0][:, 0:E],
                                 lhsT=self.xnb(cin, jt0, jt0 + NP),
                                 rhs=self.c_wrT[:, cin * E:(cin + 1) * E],
                                 start=(cin == 0), stop=False)
                    return t.matmul(out=self.pb[0][:, 0:E], lhsT=self.c_ones_colb[0:1, :],
                                    rhs=self.c_rbb[0:1, 0:E], start=False, stop=True)
                self.op("tensor", [self.bConst] + [b for cc in range(DC) for b in self.bxn(cc, jt0, jt0 + NP)],
                        [self.bBank[0]], mmr)
                self.op("vector", [self.bBank[0]], [bL8],
                        lambda v, l8=l8: v.tensor_copy(out=l8[:, 0:E], in_=self.pb[0][:, 0:E]))
                self.op("vector", [bL8], [bMR],
                        lambda v, l8=l8, mr8=mr8: v.max(out=mr8[:, 0:8], in_=l8[:, 0:E]))
                self.op("vector", [bMR], [bMR],
                        lambda v, mr8=mr8: v.memset(mr8[:, 2:8], -1e38))
                self.op("vector", [bMR, bL8], [bL2],
                        lambda v, l8=l8, mr8=mr8, l2_=l2_: v.match_replace(
                            out=l2_[:, 0:E], in_to_replace=mr8[:, 0:8],
                            in_values=l8[:, 0:E], imm_value=-1e30))
                self.op("vector", [bL2], [bMS],
                        lambda v, l2_=l2_, msk=msk: v.tensor_single_scalar(
                            out=msk[:, 0:E], in_=l2_[:, 0:E], scalar=-1e30, op=ALU.is_equal))
                self.op("scalar", [bL8], [bSG],
                        lambda a, l8=l8, sg8=sg8: a.activation(out=sg8[:, 0:E], in_=l8[:, 0:E],
                                                               func=AF.Sigmoid))
                self.op("vector", [bMS, bSG], [bCW],
                        lambda v, sg8=sg8, msk=msk, cwb_=cwb_: v.tensor_tensor(
                            out=cwb_[:, 0:E], in0=sg8[:, 0:E], in1=msk[:, 0:E], op=ALU.mult))
                self.op("tensor", [bCW, self.bConst], [self.bBankF],
                        lambda t, cwb_=cwb_: t.transpose(out=self.pbf[0:E, 0:NP], in_=cwb_[:, 0:E],
                                                         identity=self.c_identb[:]))
                self.op("scalar", [self.bBankF], [bCWR],
                        lambda a, j=j, CWR=CWR: a.copy(out=CWR[0:E, j * NP:(j + 1) * NP],
                                                       in_=self.pbf[0:E, 0:NP]))
            NQW = NQW0
            IQ = IQ0
            ICQ = IC // NQW
            bWd = [Buf(f"wd{ic}") for ic in range(IC)]
            bH = Buf("hbf")
            bSg = [Buf("sg0"), Buf("sg1")]
            hbf = self.A.bf16(IC * self.TCH * TQ, bH)
            sgb = [self.A.bf16(TQ, bSg[0]), self.A.bf16(TQ, bSg[1])]
            if 2 * DC * TH >= IC * D:
                wd_region = self.xr_half_region_bf16(oh)[:, 0:IC * D]
                wd_in_xr = True
                self.alias_seed(bWd, [self.bXR[c][oh] for c in range(DC)])
            else:
                wd_region = self.A.bf16(IC * D, bWd)
                wd_in_xr = False
            xne_region = self.xnb_half_region(oh)

            def xne(c, q0, q1, xne_region=xne_region):
                return xne_region[:, c * TH + q0:c * TH + q1]

            spill_bufs = [self.bXR[c][oh] for c in range(DC)]

            def load_wgu_q(e2, qq):
                pp = qq % 2
                nmg2 = "wg_sT" if e2 == 0 else "wgT"
                nmu2 = "wu_sT" if e2 == 0 else "wuT"
                eh2 = e2 - 1
                for cin in range(DC):
                    for mi, nm in ((0, nmg2), (1, nmu2)):
                        self.dma("gpsimd", [self.bD[nm]], [bWgu[pp][mi * DC + cin]],
                                 batch=("wgu", e2, qq),
                                 fn=lambda g, cin=cin, nm=nm, eh2=eh2, mi=mi, qq=qq, pp=pp:
                                 g.dma_start(
                                     out=wguq[pp][:, (mi * DC + cin) * IQ:
                                                  (mi * DC + cin + 1) * IQ],
                                     in_=(dd[nm][cin * NP:(cin + 1) * NP,
                                                 qq * IQ:(qq + 1) * IQ] if eh2 < 0
                                          else dd[nm][eh2, cin * NP:(cin + 1) * NP,
                                                      qq * IQ:(qq + 1) * IQ])))
            load_wgu_q(0, 0)
            for e in range(E + 1):
                nm_d = "wd_sT" if e == 0 else "wdT"
                eh = e - 1
                for ic in range(IC):
                    extra = (spill_bufs if (e == 0 and wd_in_xr) else [])
                    self.dma("gpsimd", [self.bD[nm_d]], [bWd[ic]] + extra, batch=("wd", e),
                             fn=lambda g, ic=ic, nm=nm_d, eh=eh, wd_region=wd_region: g.dma_start(
                                 out=wd_region[:, ic * D:(ic + 1) * D],
                                 in_=(dd[nm][ic * NP:(ic + 1) * NP, :] if eh < 0
                                      else dd[nm][eh, ic * NP:(ic + 1) * NP, :])))
                def emit_xne(er, xne=xne, t0h=t0h, oh=oh, CWR=CWR):
                    ehr = er - 1
                    for q in range(self.TCH):
                        q0 = q * TQ
                        self.op("tensor", [bCWR, self.bConst], [self.bBank[6]],
                                lambda t, ehr=ehr, q0=q0, CWR=CWR: t.matmul(
                                    out=self.pb[6][:, 0:TQ],
                                    lhsT=self.c_e8[0:E, ehr * NP:(ehr + 1) * NP],
                                    rhs=CWR[0:E, q0:q0 + TQ], start=True, stop=True))
                        for c in range(DC):
                            self.op("vector", [self.bBank[6]] + self.bxn(c, t0h + q0, t0h + q0 + TQ),
                                    [self.bXN[c][oh]],
                                    lambda v, c=c, q0=q0, xne=xne, t0h=t0h: v.tensor_tensor(
                                        out=xne(c, q0, q0 + TQ),
                                        in0=self.xnb(c, t0h + q0, t0h + q0 + TQ),
                                        in1=self.pb[6][:, 0:TQ], op=ALU.mult))
                if e == 1:
                    pass
                for qq in range(NQW):
                    pp = qq % 2
                    if qq + 1 < NQW:
                        load_wgu_q(e, qq + 1)
                    elif e + 1 <= E:
                        load_wgu_q(e + 1, 0)
                    for q in range(self.TCH):
                        q0 = q * TQ
                        for icl in range(ICQ):
                            ic = qq * ICQ + icl
                            kg, ku = ic % 2, 2 + ic % 2

                            def mmg(t, icl=icl, q0=q0, kg=kg, pp=pp, t0h=t0h):
                                for cin in range(DC):
                                    ii = t.matmul(
                                        out=self.pb[kg][:, 0:TQ],
                                        lhsT=wguq[pp][:, cin * IQ + icl * NP:
                                                      cin * IQ + (icl + 1) * NP],
                                        rhs=self.xnb(cin, t0h + q0, t0h + q0 + TQ),
                                        start=(cin == 0), stop=(cin == DC - 1))
                                return ii
                            self.op("tensor", bWgu[pp][0:DC] + [b for cc in range(DC)
                                    for b in self.bxn(cc, t0h + q0, t0h + q0 + TQ)],
                                    [self.bBank[kg]], mmg)

                            def mmu(t, icl=icl, q0=q0, ku=ku, e=e, xne=xne, pp=pp, t0h=t0h):
                                for cin in range(DC):
                                    if e == 0:
                                        rhs = self.xnb(cin, t0h + q0, t0h + q0 + TQ)
                                    else:
                                        rhs = xne(cin, q0, q0 + TQ)
                                    ii = t.matmul(
                                        out=self.pb[ku][:, 0:TQ],
                                        lhsT=wguq[pp][:, (DC + cin) * IQ + icl * NP:
                                                      (DC + cin) * IQ + (icl + 1) * NP],
                                        rhs=rhs, start=(cin == 0), stop=(cin == DC - 1))
                                return ii
                            ub_r = [b for cc in range(DC)
                                    for b in self.bxn(cc, t0h + q0, t0h + q0 + TQ)] \
                                if e == 0 else [self.bXN[cc][oh] for cc in range(DC)]
                            self.op("tensor", bWgu[pp][DC:2 * DC] + ub_r, [self.bBank[ku]], mmu)
                            self.op("scalar", [self.bBank[kg]], [bSg[ic % 2]],
                                    lambda a, kg=kg, ic=ic, sgb=sgb: a.activation(
                                        out=sgb[ic % 2][:, 0:TQ], in_=self.pb[kg][:, 0:TQ],
                                        func=AF.Silu))
                            self.op("vector", [self.bBank[ku], bSg[ic % 2]], [bH],
                                    lambda v, ic=ic, ku=ku, q=q, sgb=sgb, hbf=hbf: v.tensor_tensor(
                                        out=hbf[:, (ic * self.TCH + q) * TQ:
                                                 (ic * self.TCH + q + 1) * TQ],
                                        in0=sgb[ic % 2][:, 0:TQ],
                                        in1=self.pb[ku][:, 0:TQ], op=ALU.mult))
                if e + 1 <= E:
                    emit_xne(e + 1)
                for q in range(self.TCH):
                    q0 = q * TQ
                    for hc in range(DC):
                        kd = 4 + (hc % 2)

                        def mmd(t, hc=hc, kd=kd, q=q, wd_region=wd_region, hbf=hbf):
                            for ic in range(IC):
                                ii = t.matmul(out=self.pb[kd][:, 0:TQ],
                                              lhsT=wd_region[:, ic * D + hc * NP:
                                                             ic * D + (hc + 1) * NP],
                                              rhs=hbf[:, (ic * self.TCH + q) * TQ:
                                                       (ic * self.TCH + q + 1) * TQ],
                                              start=(ic == 0), stop=(ic == IC - 1))
                            return ii
                        self.op("tensor", list(bWd) + [bH], [self.bBank[kd]], mmd)
                        self.op("vector", [self.bBank[kd]] + self.bxr(hc, t0h + q0, t0h + q0 + TQ),
                                self.bxr(hc, t0h + q0, t0h + q0 + TQ),
                                lambda v, hc=hc, q0=q0, kd=kd, t0h=t0h: v.tensor_tensor(
                                    out=self.xr(hc, t0h + q0, t0h + q0 + TQ),
                                    in0=self.xr(hc, t0h + q0, t0h + q0 + TQ),
                                    in1=self.pb[kd][:, 0:TQ], op=ALU.add))
            for c in range(DC):
                self.dma("sync", [self.bD["scr_x"]],
                         [self.bXR[c][oh]] + (list(bWd) if wd_in_xr else []),
                         batch=("mxrl", h),
                         fn=lambda s, c=c, oh=oh: s.dma_start(
                             out=self.xr(c, oh * TH, (oh + 1) * TH),
                             in_=dd["scr_x"][c * NP:(c + 1) * NP, oh * TH:(oh + 1) * TH]))
            self.A.release()

    def xr_half_region_bf16(self, half):
        DC, TH = self.DC, self.TH
        v = self.XR[:, half * DC * TH:(half + 1) * DC * TH]
        return v.bitcast(BF16)

    def xr3(self, h, a, b):
        DC, TH = self.DC, self.TH
        return self.XR[:, h * DC * TH:(h + 1) * DC * TH].rearrange(
            "p (c t) -> p c t", t=TH)[:, :, a:b]

    def xnb3(self, h, a, b):
        DC, TH = self.DC, self.TH
        return self.XNB[:, h * DC * TH:(h + 1) * DC * TH].rearrange(
            "p (c t) -> p c t", t=TH)[:, :, a:b]

    def xnb_half_region(self, half):
        DC, TH = self.DC, self.TH
        return self.XNB[:, half * DC * TH:(half + 1) * DC * TH]

    def phase_attnb(self):
        dd = self.dd
        D, T, DC, TQ, TH, NH, FF, FC = self.D, self.T, self.DC, self.TQ, self.TH, self.NH, self.FF, self.FC
        self.emit_ln_rows(0, TH, fused_attnb=True)
        self.emit_ln_rows(TH, T, fused_attnb=True)
        for c in range(DC):
            for h in range(2):
                self.dma("sync", [self.bXR[c][h]], [self.bD["scr_x"]], batch="absp",
                         fn=lambda s, c=c, h=h: s.dma_start(
                             out=dd["scr_x"][c * NP:(c + 1) * NP, h * TH:(h + 1) * TH],
                             in_=self.xr(c, h * TH, (h + 1) * TH)))
        self.A.mark()
        btmp = Buf("ab_tmp")
        tmp = self.A.f32(DC * TQ, btmp)
        self.emit_apply_rows(T, 0, T, self.xr, self.bxr,
                             self.xnb, lambda c, a, b: self.bxn(c, a, b), tmp, btmp,
                             gcol=self.c_g1, becol=self.c_be1,
                             batched=(self.xr3, self.xnb3))
        self.A.release()

        self.A.mark()
        bQW = [Buf(f"qw{c}") for c in range(DC)]
        qw = self.A.bf16(DC * D, bQW)
        for cin in range(DC):
            self.dma("gpsimd", [self.bD["qkv_w"]], [bQW[cin]], batch="qw",
                     fn=lambda g, cin=cin: g.dma_start(
                         out=qw[:, cin * D:(cin + 1) * D],
                         in_=dd["qkv_w"][cin * NP:(cin + 1) * NP, 0:D]))
        bQF = Buf("qfb")
        bTQF, bTEQ = Buf("tqf"), Buf("teq")
        QFB = self.A.bf16(DC * TH, bQF)
        tqf = self.A.f32(2 * TQ, bTQF)
        teq = self.A.f32(2 * TQ, bTEQ)
        for h in range(2):
            t0h = h * TH
            for qc in range(DC):
                for q in range(self.TCH):
                    q0 = t0h + q * TQ
                    kb = q % 2

                    def mmq(t, qc=qc, q0=q0, kb=kb):
                        for cin in range(DC):
                            ii = t.matmul(out=self.pb[kb][:, 0:TQ],
                                          lhsT=qw[:, cin * D + qc * NP:cin * D + (qc + 1) * NP],
                                          rhs=self.xnb(cin, q0, q0 + TQ),
                                          start=(cin == 0), stop=(cin == DC - 1))
                        return ii
                    self.op("tensor", list(bQW) + [b for cc in range(DC) for b in self.bxn(cc, q0, q0 + TQ)],
                            [self.bBank[kb]], mmq)
                    sl = slice((q % 2) * TQ, (q % 2) * TQ + TQ)
                    self.op("vector", [self.bBank[kb], self.bConst], [bTQF],
                            lambda v, qc=qc, kb=kb, sl=sl: v.tensor_scalar(
                                out=tqf[:, sl], in0=self.pb[kb][:, 0:TQ],
                                scalar1=self.c_qkvbr[:, qc:qc + 1], scalar2=0.0,
                                op0=ALU.add, op1=ALU.min))
                    self.op("scalar", [bTQF], [bTEQ],
                            lambda a, sl=sl: a.activation(out=teq[:, sl], in_=tqf[:, sl],
                                                          func=AF.Exp))
                    self.op("vector", [self.bBank[kb], self.bConst], [bTQF],
                            lambda v, qc=qc, kb=kb, sl=sl: v.tensor_scalar(
                                out=tqf[:, sl], in0=self.pb[kb][:, 0:TQ],
                                scalar1=self.c_qkvbr[:, qc:qc + 1], scalar2=0.0,
                                op0=ALU.add, op1=ALU.max))
                    self.op("vector", [bTQF, bTEQ], [bQF],
                            lambda v, qc=qc, q=q, sl=sl: v.tensor_tensor(
                                out=QFB[:, qc * TH + q * TQ:qc * TH + (q + 1) * TQ],
                                in0=tqf[:, sl], in1=teq[:, sl], op=ALU.add))
            for qc in range(DC):
                self.dma("sync", [bQF], [self.bD["scr_qf"]], batch=("qfs", h),
                         fn=lambda s, qc=qc, h=h: s.dma_start(
                             out=dd["scr_qf"][qc * NP:(qc + 1) * NP, h * TH:(h + 1) * TH],
                             in_=QFB[:, qc * TH:(qc + 1) * TH]))
        self.A.mark()
        btmp2 = Buf("abu_tmp")
        tmp2u = self.A.f32(DC * TQ, btmp2)
        self.emit_apply_rows(0, 0, T, self.xr, self.bxr,
                             self.xr, self.bxr, tmp2u, btmp2,
                             batched=(self.xr3, self.xr3))
        self.A.release()
        self.A.release()
        self.A.mark()
        bKF, bVB = Buf("kfb"), Buf("vb")
        bTKF, bTKE = Buf("tkf"), Buf("tke")
        bQKVBB = Buf("qkvbb")
        c_qkvbb = self.A.bf16(3 * self.D, bQKVBB)
        self.dma("gpsimd", [self.bD["qkv_b"]], [bQKVBB],
                 fn=lambda g: g.dma_start(out=c_qkvbb[0:1, :], in_=dd["qkv_b"]))
        bKVWA = [Buf(f"kvwa{c}") for c in range(DC)]
        kvw_all = self.A.bf16(DC * 2 * D, bKVWA)
        for cin in range(DC):
            self.dma("gpsimd", [self.bD["qkv_w"]], [bKVWA[cin]], batch="kvw",
                     fn=lambda g, cin=cin: g.dma_start(
                         out=kvw_all[:, cin * 2 * D:(cin + 1) * 2 * D],
                         in_=dd["qkv_w"][cin * NP:(cin + 1) * NP, D:3 * D]))
        KFB = self.A.bf16(self.TTH * D, bKF)
        VB = self.A.bf16(self.TTH * NH * (self.HD + 1), bVB)
        VROW = NH * (self.HD + 1)
        tkf = self.A.f32(TQ, bTKF)
        tke = self.A.f32(TQ, bTKE)
        NKV = 2 * D // TQ
        kvacc = [4, 5, 6]
        bKVA = [self.bBank[4], self.bBank[5], self.bBank[6]]
        PACK = 7
        for h in range(2):
            t0h = h * TH
            self.op("vector", [bVB] if h == 0 else [bVB], [bVB],
                    lambda v: v.memset(VB[:], 1.0)) if h == 0 else None
            for j in range(self.TTH):
                jt0 = t0h + j * NP
                for cin in range(DC):
                    def mmkv(t, cin=cin, jt0=jt0):
                        for sidx in range(NKV):
                            ii = t.matmul(out=self.pb[sidx][:, 0:TQ],
                                          lhsT=self.xnb(cin, jt0, jt0 + NP),
                                          rhs=kvw_all[:, cin * 2 * D + sidx * TQ:
                                                      cin * 2 * D + (sidx + 1) * TQ],
                                          start=(cin == 0), stop=False)
                        return ii
                    self.op("tensor", [bKVWA[cin]] + self.bxn(cin, jt0, jt0 + NP),
                            [self.bBank[s] for s in range(NKV)], mmkv)

                def mmbias(t, jt0=jt0):
                    for sidx in range(NKV):
                        ii = t.matmul(out=self.pb[sidx][:, 0:TQ], lhsT=self.c_ones_colb[0:1, :],
                                      rhs=c_qkvbb[0:1, D + sidx * TQ:D + (sidx + 1) * TQ],
                                      start=False, stop=True)
                    return ii
                self.op("tensor", [self.bConst, bQKVBB], [self.bBank[s] for s in range(NKV)], mmbias)
                nks = D // TQ
                for sidx in range(NKV):
                    if sidx < nks:
                        sl = slice(0, TQ)
                        self.op("vector", [self.bBank[sidx]], [bTKF],
                                lambda v, sidx=sidx, sl=sl: v.tensor_scalar_min(
                                    out=tkf[:, sl], in0=self.pb[sidx][:, 0:TQ], scalar1=0.0))
                        self.op("scalar", [bTKF], [bTKE],
                                lambda a, sl=sl: a.activation(out=tke[:, sl], in_=tkf[:, sl],
                                                              func=AF.Exp))
                        self.op("vector", [self.bBank[sidx], bTKE], [bKF],
                                lambda v, sidx=sidx, j=j, sl=sl: v.scalar_tensor_tensor(
                                    out=KFB[:, j * D + sidx * TQ:j * D + (sidx + 1) * TQ],
                                    in0=self.pb[sidx][:, 0:TQ], scalar=0.0,
                                    in1=tke[:, sl], op0=ALU.max, op1=ALU.add))
                    else:
                        v0 = (sidx - nks) * TQ
                        hpt = TQ // self.HD
                        h0 = v0 // self.HD

                        def evv(a, sidx=sidx, j=j, h0=h0, hpt=hpt):
                            src = self.pb[sidx][:, 0:TQ].rearrange("p (hh d) -> p hh d", d=self.HD)
                            dst = VB[:, j * VROW + h0 * (self.HD + 1):
                                     j * VROW + (h0 + hpt) * (self.HD + 1)]
                            dst = dst.rearrange("p (hh d) -> p hh d", d=self.HD + 1)[:, :, 0:self.HD]
                            return a.copy(out=dst, in_=src)
                        self.op("scalar", [self.bBank[sidx]], [bVB], evv)
                for hd in range(NH):
                    kb = kvacc[hd // PACK]
                    col = (hd % PACK) * (self.HD + 1)
                    first_in_bank = (hd % PACK == 0)
                    last_in_bank = (hd % PACK == PACK - 1) or (hd == NH - 1)
                    self.op("tensor", [bKF, bVB],
                            [bKVA[hd // PACK]],
                            lambda t, hd=hd, j=j, kb=kb, col=col, h=h,
                            fb=first_in_bank, lb=last_in_bank: t.matmul(
                                out=self.pb[kb][0:self.HD, col:col + self.HD + 1],
                                lhsT=KFB[:, j * D + hd * self.HD:j * D + (hd + 1) * self.HD],
                                rhs=VB[:, j * VROW + hd * (self.HD + 1):
                                       j * VROW + (hd + 1) * (self.HD + 1)],
                                start=(h == 0 and j == 0 and fb),
                                stop=(h == 1 and j == self.TTH - 1 and lb)))
        bKVB = Buf("kvb")
        KVB = self.A.bf16(NH * (self.HD + 1), bKVB)
        for hd in range(NH):
            kb = kvacc[hd // PACK]
            col = (hd % PACK) * (self.HD + 1)
            self.op("scalar", [bKVA[hd // PACK]], [bKVB],
                    lambda a, hd=hd, kb=kb, col=col: a.copy(
                        out=KVB[0:self.HD, hd * (self.HD + 1):(hd + 1) * (self.HD + 1)],
                        in_=self.pb[kb][0:self.HD, col:col + self.HD + 1]))
        self.A.release()
        self.A.mark()
        bQHA = Buf("qha")
        bYB = Buf("yb")
        bRD = [Buf(f"rden{p}") for p in range(3)]
        bRD2 = [Buf(f"rden2{p}") for p in range(3)]
        bRB = [Buf("rbc0"), Buf("rbc1")]
        bOW = [Buf(f"ow{c}") for c in range(DC)]
        QHA = self.A.bf16(NH * TH, bQHA)
        YB = self.A.bf16(DC * TH, bYB)
        rden = self.A.f32(TQ, bRD)
        rden2 = self.A.f32(TQ, bRD2)
        rbc = self.A.f32(2 * TQ, bRB)
        ow = self.A.bf16(DC * D, bOW)
        HD = self.HD
        for cin in range(DC):
            self.dma("gpsimd", [self.bD["out_w"]], [bOW[cin]], batch="ow",
                     fn=lambda g, cin=cin: g.dma_start(
                         out=ow[:, cin * D:(cin + 1) * D],
                         in_=dd["out_w"][cin * NP:(cin + 1) * NP, :]))
        for h in range(2):
            t0h = h * TH
            for hd in range(NH):
                self.dma("sync", [self.bD["scr_qf"]], [bQHA], batch=("qha", h),
                         fn=lambda s, hd=hd, h=h: s.dma_start(
                             out=QHA[0:HD, hd * TH:(hd + 1) * TH],
                             in_=dd["scr_qf"][hd * HD:(hd + 1) * HD, h * TH:(h + 1) * TH]))
            tasks = [(hd, q) for hd in range(NH) for q in range(self.TCH)]

            def mmy_k(k):
                hd, q = tasks[k]
                rb0 = (hd * HD) % NP
                q0 = q * TQ
                kb, kb2, p = k % 4, 4 + k % 3, k % 3

                def mmy(t, hd=hd, rb0=rb0, q0=q0, kb=kb, kb2=kb2, p=p):
                    t.matmul(
                        out=self.pb[kb][rb0:rb0 + HD, 0:TQ],
                        lhsT=KVB[0:HD, hd * (HD + 1):hd * (HD + 1) + HD],
                        rhs=QHA[0:HD, hd * TH + q0:hd * TH + q0 + TQ], start=True, stop=True)
                    return t.matmul(
                        out=self.pb[kb2][32 * p:32 * p + 1, 0:TQ],
                        lhsT=KVB[0:HD, hd * (HD + 1) + HD:(hd + 1) * (HD + 1)],
                        rhs=QHA[0:HD, hd * TH + q0:hd * TH + q0 + TQ], start=True, stop=True)
                self.op("tensor", [bKVB, bQHA], [self.bBank[kb], self.bBank[kb2]], mmy)

            n_tasks = len(tasks)

            def s_maxlnexp(k):
                p, kb2 = k % 3, 4 + k % 3
                self.op("vector", [self.bBank[kb2]], [bRD[p]],
                        lambda v, kb2=kb2, p=p: v.tensor_scalar_max(
                            out=rden[32 * p:32 * p + 1, 0:TQ],
                            in0=self.pb[kb2][32 * p:32 * p + 1, 0:TQ],
                            scalar1=1e-6))
                self.op("scalar", [bRD[p]], [bRD2[p]],
                        lambda a, p=p: a.activation(
                            out=rden2[32 * p:32 * p + 1, 0:TQ],
                            in_=rden[32 * p:32 * p + 1, 0:TQ], func=AF.Ln))
                self.op("scalar", [bRD2[p], self.bConst], [bRD[p]],
                        lambda a, p=p: a.activation(
                            out=rden[32 * p:32 * p + 1, 0:TQ],
                            in_=rden2[32 * p:32 * p + 1, 0:TQ],
                            func=AF.Exp, scale=self.ck(-1.0, 32 * p, 1)))

            def s_tail(k):
                hd, q = tasks[k]
                rb0 = (hd * HD) % NP
                q0 = q * TQ
                kb, kb2, p, pc = k % 4, 4 + k % 3, k % 3, k % 2
                self.op("tensor", [bRD[p], self.bConst], [self.bBank[kb2]],
                        lambda t, kb2=kb2, rb0=rb0, p=p: t.matmul(
                            out=self.pb[kb2][rb0:rb0 + HD, 0:TQ],
                            lhsT=self.c_ones_col[32 * p:32 * p + 1, 0:HD],
                            rhs=rden[32 * p:32 * p + 1, 0:TQ], start=True, stop=True))
                self.op("scalar", [self.bBank[kb2]], [bRB[pc]],
                        lambda a, kb2=kb2, rb0=rb0, pc=pc: a.copy(
                            out=rbc[rb0:rb0 + HD, pc * TQ:pc * TQ + TQ],
                            in_=self.pb[kb2][rb0:rb0 + HD, 0:TQ]))
                self.op("vector", [self.bBank[kb], bRB[pc]], [bYB],
                        lambda v, hd=hd, rb0=rb0, q0=q0, kb=kb, pc=pc: v.tensor_tensor(
                            out=YB[rb0:rb0 + HD, ((hd * HD) // NP) * TH + q0:
                                   ((hd * HD) // NP) * TH + q0 + TQ],
                            in0=self.pb[kb][rb0:rb0 + HD, 0:TQ],
                            in1=rbc[rb0:rb0 + HD, pc * TQ:pc * TQ + TQ], op=ALU.mult))

            mmy_k(0)
            mmy_k(1)
            s_maxlnexp(0)
            for i in range(n_tasks):
                if i + 2 < n_tasks:
                    mmy_k(i + 2)
                if i + 1 < n_tasks:
                    s_maxlnexp(i + 1)
                s_tail(i)
            for cout in range(DC):
                for q in range(self.TCH):
                    q0 = t0h + q * TQ
                    kb = 4 + (q % 2)

                    def mmo(t, cout=cout, q0=q0, kb=kb, t0h=t0h):
                        for cin in range(DC):
                            ii = t.matmul(out=self.pb[kb][:, 0:TQ],
                                          lhsT=ow[:, cin * D + cout * NP:cin * D + (cout + 1) * NP],
                                          rhs=YB[:, cin * TH + q0 - t0h:cin * TH + q0 - t0h + TQ],
                                          start=(cin == 0), stop=(cin == DC - 1))
                        return ii
                    self.op("tensor", list(bOW) + [bYB], [self.bBank[kb]], mmo)
                    self.op("vector", [self.bBank[kb], self.bConst] + self.bxr(cout, q0, q0 + TQ),
                            self.bxr(cout, q0, q0 + TQ),
                            lambda v, cout=cout, q0=q0, kb=kb: v.scalar_tensor_tensor(
                                out=self.xr(cout, q0, q0 + TQ), in0=self.pb[kb][:, 0:TQ],
                                scalar=self.c_outb[:, cout:cout + 1],
                                in1=self.xr(cout, q0, q0 + TQ), op0=ALU.add, op1=ALU.add))
        self.A.release()

        self.A.mark()
        btmp = Buf("ab2_tmp")
        tmp = self.A.f32(DC * TQ, btmp)
        for h in range(2):
            self.emit_ln_rows(h * TH, (h + 1) * TH)
            self.emit_apply_rows(0, h * TH, (h + 1) * TH, self.xr, self.bxr,
                                 self.xnb, lambda c, a, b: self.bxn(c, a, b), tmp, btmp,
                                 gcol=self.c_g2, becol=self.c_be2,
                                 batched=(self.xr3, self.xnb3))
        NFQ = 8 if FC % 8 == 0 else 4
        FQC = FC // NFQ
        bW1 = [[Buf(f"w1q{p}_{c}") for c in range(DC)] for p in range(2)]
        bW2 = [[Buf(f"w2q{p}_{c}") for c in range(FC // NFQ)] for p in range(2)]
        bHQ = Buf("hq")
        w1q = [self.A.bf16(DC * FQC * NP, bW1[0]), self.A.bf16(DC * FQC * NP, bW1[1])]
        w2q = [self.A.bf16(FQC * D, bW2[0]), self.A.bf16(FQC * D, bW2[1])]
        hq = self.A.bf16(FQC * TQ, bHQ)
        FQW = FQC * NP
        for i in range(self.TC):
            t0, t1 = i * TQ, (i + 1) * TQ
            for fq in range(NFQ):
                pp = fq % 2
                for cin in range(DC):
                    self.dma("gpsimd", [self.bD["w1"]], [bW1[pp][cin]], batch=("w1", i, fq),
                             fn=lambda g, cin=cin, fq=fq, pp=pp: g.dma_start(
                                 out=w1q[pp][:, cin * FQW:(cin + 1) * FQW],
                                 in_=dd["w1"][cin * NP:(cin + 1) * NP,
                                              fq * FQW:(fq + 1) * FQW]))
                for fc in range(FQC):
                    self.dma("gpsimd", [self.bD["w2"]], [bW2[pp][fc]], batch=("w2", i, fq),
                             fn=lambda g, fc=fc, fq=fq, pp=pp: g.dma_start(
                                 out=w2q[pp][:, fc * D:(fc + 1) * D],
                                 in_=dd["w2"][(fq * FQC + fc) * NP:(fq * FQC + fc + 1) * NP, :]))
                for fc in range(FQC):
                    kb = fc % 2

                    def mm1(t, fc=fc, t0=t0, t1=t1, kb=kb, pp=pp):
                        for cin in range(DC):
                            ii = t.matmul(out=self.pb[kb][:, 0:TQ],
                                          lhsT=w1q[pp][:, cin * FQW + fc * NP:
                                                       cin * FQW + (fc + 1) * NP],
                                          rhs=self.xnb(cin, t0, t1),
                                          start=(cin == 0), stop=(cin == DC - 1))
                        return ii
                    self.op("tensor", list(bW1[pp]) + [b for cc in range(DC) for b in self.bxn(cc, t0, t1)],
                            [self.bBank[kb]], mm1)
                    self.op("scalar", [self.bBank[kb], self.bConst], [bHQ],
                            lambda a, fc=fc, fq=fq, kb=kb: a.activation(
                                out=hq[:, fc * TQ:(fc + 1) * TQ], in_=self.pb[kb][:, 0:TQ],
                                func=AF.Gelu,
                                bias=self.c_b1[:, fq * FQC + fc:fq * FQC + fc + 1]))
                for dc_ in range(DC):
                    kd = 4 + (dc_ % 2)

                    def mm2(t, dc_=dc_, kd=kd, pp=pp):
                        for fc in range(FQC):
                            ii = t.matmul(out=self.pb[kd][:, 0:TQ],
                                          lhsT=w2q[pp][:, fc * D + dc_ * NP:fc * D + (dc_ + 1) * NP],
                                          rhs=hq[:, fc * TQ:(fc + 1) * TQ],
                                          start=(fc == 0), stop=(fc == FQC - 1))
                        return ii
                    self.op("tensor", list(bW2[pp]) + [bHQ], [self.bBank[kd]], mm2)
                    if fq == 0:
                        self.op("vector", [self.bBank[kd], self.bConst] + self.bxr(dc_, t0, t1),
                                self.bxr(dc_, t0, t1),
                                lambda v, dc_=dc_, t0=t0, t1=t1, kd=kd: v.scalar_tensor_tensor(
                                    out=self.xr(dc_, t0, t1), in0=self.pb[kd][:, 0:TQ],
                                    scalar=self.c_b2[:, dc_:dc_ + 1],
                                    in1=self.xr(dc_, t0, t1), op0=ALU.add, op1=ALU.add))
                    else:
                        self.op("vector", [self.bBank[kd]] + self.bxr(dc_, t0, t1),
                                self.bxr(dc_, t0, t1),
                                lambda v, dc_=dc_, t0=t0, t1=t1, kd=kd: v.tensor_tensor(
                                    out=self.xr(dc_, t0, t1), in0=self.xr(dc_, t0, t1),
                                    in1=self.pb[kd][:, 0:TQ], op=ALU.add))
        self.A.release()
        self.A.mark()
        bXL = [Buf("xl0"), Buf("xl1")]
        xl = [self.A.f32(TH, bXL[0]), self.A.f32(TH, bXL[1])]
        for c in range(DC):
            for h in range(2):
                st = (c * 2 + h) % 2
                self.dma("sync", [self.bD["scr_x"], bXL[st]], [bXL[st]],
                         lambda s, c=c, h=h, st=st: s.dma_start(
                             out=xl[st][:], in_=dd["scr_x"][c * NP:(c + 1) * NP,
                                                            h * TH:(h + 1) * TH]))
                self.op("vector", [bXL[st]] + [self.bXR[c][h]], [self.bXR[c][h]],
                        lambda v, c=c, h=h, st=st: v.tensor_tensor(
                            out=self.xr(c, h * TH, (h + 1) * TH),
                            in0=self.xr(c, h * TH, (h + 1) * TH),
                            in1=xl[st][:], op=ALU.add))
        self.A.release()

    def phase_store(self):
        dd, DC, TH = self.dd, self.DC, self.TH
        nst = 0
        for c in range(DC):
            for h in range(2):
                self.dma("sync", [self.bXR[c][h]], [self.bD["outT"]],
                         lambda s, c=c, h=h: s.dma_start(
                             out=self.out_d[c * NP:(c + 1) * NP, h * TH:(h + 1) * TH],
                             in_=self.xr(c, h * TH, (h + 1) * TH)))
                nst += 1
        target = self.cLD.n

        def fin(s, cache):
            s.wait_ge(self.cLD.h, target)
        self.pend["sync"].append(fin)


def _prep_inputs(cfg, ins):
    D, T, NST, R, E = cfg["D"], cfg["T"], cfg["NST"], cfg["R"], cfg["E"]
    I, FF = cfg["I"], cfg["FF"]
    DC = D // NP
    t = lambda a: np.ascontiguousarray(np.asarray(a).T)
    r = lambda a, k: np.ascontiguousarray(np.asarray(a).reshape(k, NP).T)

    def r2(a, k0, k1):
        a = np.asarray(a).reshape(k0, NP, k1)
        return np.ascontiguousarray(a.transpose(1, 0, 2).reshape(NP, k0 * k1))

    out = {}
    out["w_inT"] = t(ins["L0_attn_a_W_in"])
    out["conv_w_r"] = r2(ins["L0_attn_a_conv_w"], DC, 3)
    wx = t(ins["L0_attn_a_W_x"])
    wxp = np.zeros((D, NP), np.float32)
    wxp[:, 0:NST] = wx[:, R:R + NST]
    wxp[:, 32:32 + NST] = wx[:, R + NST:R + 2 * NST]
    wxp[:, 64:64 + R] = wx[:, 0:R]
    out["w_xT"] = wxp
    out["w_dtT"] = t(ins["L0_attn_a_W_dt"])
    out["b_dt_r"] = r(ins["L0_attn_a_b_dt"], DC)
    out["na_r"] = r2(ins["L0_attn_a_A_log"], DC, NST)
    out["d_param_r"] = r(ins["L0_attn_a_D_param"], DC)
    out["w_cT"] = t(ins["L0_attn_a_W_C"])
    out["w_outT"] = t(ins["L0_attn_a_W_out"])
    out["qkv_w"] = np.ascontiguousarray(ins["L0_attn_b_qkv_w"])
    out["qkv_b"] = np.asarray(ins["L0_attn_b_qkv_b"]).reshape(1, -1)
    out["qkv_b_r"] = r(ins["L0_attn_b_qkv_b"], 3 * D // NP)
    out["out_w"] = np.ascontiguousarray(ins["L0_attn_b_out_w"])
    out["out_b_r"] = r(ins["L0_attn_b_out_b"], DC)
    out["w1"] = np.ascontiguousarray(ins["L0_attn_b_w1"])
    out["b1_r"] = r(ins["L0_attn_b_b1"], FF // NP)
    out["w2"] = np.ascontiguousarray(ins["L0_attn_b_w2"])
    out["b2_r"] = r(ins["L0_attn_b_b2"], DC)
    out["g1_r"] = r(ins["L0_attn_b_g1"], DC)
    out["be1_r"] = r(ins["L0_attn_b_be1"], DC)
    out["g2_r"] = r(ins["L0_attn_b_g2"], DC)
    out["be2_r"] = r(ins["L0_attn_b_be2"], DC)
    out["wg_sT"] = t(ins["L0_ffn_Wg_s"])
    out["wu_sT"] = t(ins["L0_ffn_Wu_s"])
    out["wd_sT"] = t(ins["L0_ffn_Wd_s"])
    out["wgT"] = np.ascontiguousarray(np.asarray(ins["L0_ffn_Wg"]).transpose(0, 2, 1))
    out["wuT"] = np.ascontiguousarray(np.asarray(ins["L0_ffn_Wu"]).transpose(0, 2, 1))
    out["wdT"] = np.ascontiguousarray(np.asarray(ins["L0_ffn_Wd"]).transpose(0, 2, 1))
    out["wrT"] = t(ins["L0_ffn_Wr"])
    out["rb"] = np.asarray(ins["L0_ffn_rb"]).reshape(1, -1)
    out["ident"] = np.eye(NP, dtype=np.float32)
    e2 = np.zeros((2, 2 * NP), np.float32)
    e2[0, 0:NP] = 1.0
    e2[1, NP:2 * NP] = 1.0
    out["e2"] = e2
    out["ones_col"] = np.ones((1, NP), np.float32)
    out["ones_k"] = np.ones((NP, 1), np.float32)
    e8 = np.zeros((E, E * NP), np.float32)
    for n in range(E):
        e8[n, n * NP:(n + 1) * NP] = 1.0
    out["e8"] = e8
    for k in out:
        out[k] = np.ascontiguousarray(np.asarray(out[k], dtype=np.float32))
    return out


def kernel(**inputs):
    cfg = default_cfg()
    mk = MK(cfg)
    nc = mk.build()
    shared = _prep_inputs(cfg, inputs)
    x = np.asarray(inputs["L0_attn_a_x"], np.float32)
    B = x.shape[0]
    in_maps = []
    for b in range(B):
        m = dict(shared)
        m["xT"] = np.ascontiguousarray(x[b].T)
        in_maps.append(m)
    res = run_bass_kernel_spmd(nc, in_maps, core_ids=list(range(B)))
    outs = [np.ascontiguousarray(r["outT"].T) for r in res.results]
    return np.stack(outs, axis=0)


# version 19 (final)
# speedup vs baseline: 1.1610x; 1.1610x over previous
import numpy as np
import concourse.bass as bass
import concourse.mybir as mybir
from concourse.bass_utils import run_bass_kernel_spmd
from contextlib import ExitStack

F32 = mybir.dt.float32
BF16 = mybir.dt.bfloat16
AF = mybir.ActivationFunctionType
ALU = mybir.AluOpType
AX = mybir.AxisListType.X
NP = 128

ENG_CHAN = {"tensor": "cPE", "vector": "cDV", "scalar": "cAC", "gpsimd": "cGP"}
DMA_CHAN = {"sync": "cLD", "gpsimd": "cLW"}


def default_cfg():
    return dict(T=2048, D=1024, NST=16, R=64, HD=64, I=2048, E=8, FF=4096,
                parts=("mamba", "moe1", "attnb", "moe2"))


class Chan:
    def __init__(self, es, nc, name):
        self.h = es.enter_context(nc.semaphore(name))
        self.n = 0


class Buf:
    __slots__ = ("name", "w", "readers")

    def __init__(self, name):
        self.name = name
        self.w = None
        self.readers = []


class Arena:

    def __init__(self, tens, nbytes):
        self.t = tens
        self.nbytes = nbytes
        self.pos = 0
        self.stack = []
        self.hwm = 0
        self.live = []
        self.ghost = []

    def f32(self, cols, bufs=()):
        off = self._take(cols * 4, 4, bufs)
        return self.t[:, off // 4: off // 4 + cols]

    def bf16(self, cols, bufs=()):
        cols2 = (cols + 1) // 2 * 2
        off = self._take(cols2 * 2, 4, bufs)
        return self.t[:, off // 4: off // 4 + cols2 // 2].bitcast(BF16)[:, 0:cols]

    def _take(self, nb, align, bufs):
        if isinstance(bufs, Buf):
            bufs = [bufs]
        bufs = list(bufs)
        p = (self.pos + align - 1) // align * align
        assert p + nb <= self.nbytes, f"SBUF arena overflow: {p + nb} > {self.nbytes}"
        self.pos = p + nb
        self.hwm = max(self.hwm, self.pos)
        for (g0, g1, gbufs) in self.ghost:
            if g0 < p + nb and p < g1:
                for gb in gbufs:
                    ms = ([gb.w] if gb.w else []) + gb.readers
                    for b in bufs:
                        b.readers.extend(ms)
        self.live.append((p, p + nb, bufs))
        return p

    def mark(self):
        self.stack.append(self.pos)

    def release(self):
        newpos = self.stack.pop()
        keep = []
        for rec in self.live:
            (keep if rec[0] < newpos else self.ghost).append(rec)
        self.live = keep
        self.pos = newpos


class MK:
    def __init__(self, cfg):
        self.cfg = cfg
        self.T = cfg["T"]; self.D = cfg["D"]; self.NST = cfg["NST"]; self.R = cfg["R"]
        self.I = cfg["I"]; self.E = cfg["E"]; self.FF = cfg["FF"]; self.HD = cfg["HD"]
        self.DC = self.D // NP
        self.TH = self.T // 2
        self.NH = self.D // self.HD
        self.IC = self.I // NP
        self.FC = self.FF // NP
        self.TQ = min(512, self.TH)
        self.TC = self.T // self.TQ
        self.TCH = self.TH // self.TQ
        self.TT = self.T // NP
        self.TTH = self.TH // NP
        self.nc = bass.Bass("TRN2", target_bir_lowering=False)
        self.es = ExitStack()
        self.plan_idx = 0
        self.pend = {k: [] for k in ("tensor", "vector", "scalar", "sync", "gpsimd")}
        self.waited = {k: {} for k in self.pend}
        self.drained = {k: -1 for k in self.pend}
        self.emitted = {k: -1 for k in self.pend}

    def op(self, eng, reads, writes, fn):
        assert eng in ("tensor", "vector", "scalar", "gpsimd")
        self._queue(eng, ENG_CHAN[eng], reads, writes, fn, inc_amt=1, batch=None)

    def dma(self, queue, reads, writes, fn, batch=None):
        if batch is None:
            batch = ("auto", self.plan_idx)
        self._queue(queue, DMA_CHAN[queue], reads, writes, fn, inc_amt=16, batch=batch)

    def _queue(self, eng, chan_name, reads, writes, fn, inc_amt, batch):
        idx = self.plan_idx
        self.plan_idx += 1
        ch = getattr(self, chan_name)
        waits = []
        need_drain = False

        def consider(ms):
            nonlocal need_drain
            if ms is None:
                return
            mch, mref, msrc, midx = ms
            if msrc == eng and mch == chan_name and inc_amt == 1:
                if midx > self.drained[eng]:
                    need_drain = True
            elif inc_amt == 16 and mch == chan_name:
                pass
            else:
                waits.append((mch, mref))

        for b in reads:
            consider(b.w)
        for b in writes:
            consider(b.w)
            for r in b.readers:
                consider(r)
        if inc_amt == 16:
            if getattr(ch, "cur_batch", None) != batch:
                waits.append((chan_name, [ch.n]))
                ch.cur_batch = batch
                ch.batch_ref = [0]
        if need_drain:
            self.drained[eng] = idx - 1
        ch.n += inc_amt
        if inc_amt == 16:
            ch.batch_ref[0] = ch.n
            ms_new = (chan_name, ch.batch_ref, eng, idx)
        else:
            ms_new = (chan_name, [ch.n], eng, idx)
        for b in writes:
            b.w = ms_new
            b.readers = []
        for b in reads:
            b.readers.append(ms_new)
        chans = {cn: getattr(self, cn) for cn in set([cn for cn, _ in waits] + [chan_name])}

        def emit(e, cache):
            for cn, ref in waits:
                v = ref[0]
                if cache.get(cn, -1) < v:
                    e.wait_ge(chans[cn].h, v)
                    cache[cn] = v
            if need_drain:
                e.drain()
            last = fn(e)
            assert last is not None, f"op fn returned None (plan {idx})"
            if getattr(self, "oplog", None) is not None:
                self.oplog.append((idx, eng, getattr(last, "name", None) if not callable(getattr(last, "name", None)) else last.name(),
                                   [(cn, r[0]) for cn, r in waits],
                                   [b.name for b in reads], [b.name for b in writes]))
            last.then_inc(chans[chan_name].h, inc_amt)
        self.pend[eng].append(emit)

    def flush(self):
        for eng in ("sync", "gpsimd", "vector", "scalar", "tensor"):
            lst = self.pend[eng]
            if not lst:
                continue
            self.pend[eng] = []
            dec = getattr(self.blk, eng)

            def body(e, lst=lst):
                cache = {}
                for f in lst:
                    f(e, cache)
            dec(body)

    def declare_dram(self):
        nc = self.nc
        D, T, R, NST, I, E, FF, DC = self.D, self.T, self.R, self.NST, self.I, self.E, self.FF, self.DC
        dd = {}

        def din(name, shape, dt=F32):
            dd[name] = nc.dram_tensor(name, shape, dt, kind="ExternalInput").ap()

        din("xT", [D, T])
        din("w_inT", [D, 2 * D])
        din("conv_w_r", [NP, DC * 3])
        din("w_xT", [D, NP])
        din("w_dtT", [R, D])
        din("b_dt_r", [NP, DC])
        din("na_r", [NP, DC * NST])
        din("d_param_r", [NP, DC])
        din("w_cT", [NST, D])
        din("w_outT", [D, D])
        din("qkv_w", [D, 3 * D])
        din("qkv_b", [1, 3 * D])
        din("qkv_b_r", [NP, 3 * D // NP])
        din("out_w", [D, D])
        din("out_b_r", [NP, DC])
        din("w1", [D, FF])
        din("b1_r", [NP, FF // NP])
        din("w2", [FF, D])
        din("b2_r", [NP, DC])
        din("g1_r", [NP, DC]); din("be1_r", [NP, DC])
        din("g2_r", [NP, DC]); din("be2_r", [NP, DC])
        din("wg_sT", [D, I]); din("wu_sT", [D, I]); din("wd_sT", [I, D])
        din("wgT", [E, D, I]); din("wuT", [E, D, I]); din("wdT", [E, I, D])
        din("wrT", [D, E])
        din("rb", [1, E])
        din("ident", [NP, NP])
        din("ones_col", [1, NP])
        din("ones_k", [NP, 1])
        din("e8", [E, E * NP])
        din("e2", [2, 2 * NP])
        for name, shape, dt in [("scr_z", [D, T], F32), ("scr_xs", [D, T], F32),
                                ("scr_x", [D, T], F32), ("scr_qf", [D, T], BF16),
                                ("scr_xs_bf", [D, T], BF16), ("scr_b", [NST, T], BF16)]:
            dd[name] = nc.dram_tensor(name, shape, dt, kind="Internal").ap()
        self.out_d = nc.dram_tensor("outT", [D, T], F32, kind="ExternalOutput").ap()
        self.dd = dd
        self.bD = {k: Buf("d_" + k) for k in list(dd.keys()) + ["outT"]}

    def declare_onchip(self):
        nc, es = self.nc, self.es
        D, T, DC = self.D, self.T, self.DC
        cols = 52048 if T > 512 else 26000
        self.sb = es.enter_context(nc.sbuf_tensor("arena", [NP, cols], F32))
        self.A = Arena(self.sb, cols * 4)
        A = self.A
        self.XR = A.f32(DC * T)
        self.XNB = A.bf16(DC * T)
        self.c_identb = A.bf16(NP)
        self.c_identf = A.f32(NP)
        self.c_ones_col = A.f32(NP)
        self.c_ones_colb = A.bf16(NP)
        self.c_ones_k = A.bf16(2)
        self.c_e8 = A.bf16(self.E * NP)
        self.c_conv = A.f32(DC * 3)
        self.c_bdt = A.f32(DC)
        self.c_dpar = A.f32(DC)
        self.c_na = A.f32(DC * self.NST)
        self.c_nb = A.f32(DC * self.NST)
        self.c_g1 = A.f32(DC); self.c_be1 = A.f32(DC)
        self.c_g2 = A.f32(DC); self.c_be2 = A.f32(DC)
        self.c_outb = A.f32(DC)
        self.c_b2 = A.f32(DC)
        self.c_b1 = A.f32(self.FC)
        self.c_rbb = A.bf16(self.E)
        self.c_wrT = A.bf16(DC * self.E)
        self.c_qkvbr = A.f32(3 * D // NP)
        self.c_k = A.f32(4)
        self.ck_map = {1e-5: 0, 1.0: 1, -1.0: 2, -0.5: 3}
        self.rows2 = A.f32(2 * T)
        self.c_e2 = A.f32(2 * NP)

        self.pb = [es.enter_context(nc.psum_tensor(f"pb{i}", [NP, 512], F32)) for i in range(7)]
        self.pbf = es.enter_context(nc.psum_tensor("pbf", [NP, 1024], BF16))
        for nm in ("cLD", "cLW", "cPE", "cDV", "cAC", "cGP"):
            setattr(self, nm, Chan(es, nc, nm))
        self.bXR = [[Buf(f"xr{c}h{h}") for h in range(2)] for c in range(DC)]
        self.bXN = [[Buf(f"xn{c}h{h}") for h in range(2)] for c in range(DC)]
        self.bRows = Buf("rows2")
        self.bConst = Buf("const")
        self.bBank = [Buf(f"pb{i}") for i in range(7)]
        self.bBankF = Buf("pbf")

    def alias_seed(self, dst_bufs, src_bufs):
        for sb in src_bufs:
            ms = ([sb.w] if sb.w else []) + sb.readers
            for b in dst_bufs:
                b.readers.extend(ms)

    def ck(self, v, base=0, cnt=NP):
        i = self.ck_map[v]
        return self.c_k[base:base + cnt, i:i + 1]

    def xr(self, c, t0, t1):
        TH, DC = self.TH, self.DC
        h = t0 // TH
        assert (t1 - 1) // TH == h, (t0, t1)
        base = (h * DC + c) * TH
        return self.XR[:, base + t0 - h * TH: base + t1 - h * TH]

    def xnb(self, c, t0, t1):
        TH, DC = self.TH, self.DC
        h = t0 // TH
        assert (t1 - 1) // TH == h
        base = (h * DC + c) * TH
        return self.XNB[:, base + t0 - h * TH: base + t1 - h * TH]

    def bxr(self, c, t0, t1):
        return [self.bXR[c][h] for h in range(t0 // self.TH, (t1 - 1) // self.TH + 1)]

    def bxn(self, c, t0, t1):
        return [self.bXN[c][h] for h in range(t0 // self.TH, (t1 - 1) // self.TH + 1)]

    def build(self):
        self.declare_dram()
        with self.es:
            self.declare_onchip()
            self.blk = self.es.enter_context(self.nc.Block())
            self.phase_load()
            parts = self.cfg["parts"]
            if "mamba" in parts:
                self.phase_mamba()
            if "moe1" in parts:
                self.phase_moe()
            if "attnb" in parts:
                self.phase_attnb()
            if "moe2" in parts:
                self.phase_moe()
            self.phase_store()
            self.flush()
        return self.nc

    def phase_load(self):
        dd = self.dd
        DC, TH, NST, E = self.DC, self.TH, self.NST, self.E
        for c in range(DC):
            for h in range(2):
                self.dma("sync", [self.bD["xT"]], [self.bXR[c][h]], batch="x0",
                         fn=lambda s, c=c, h=h: s.dma_start(
                             out=self.xr(c, h * TH, (h + 1) * TH),
                             in_=dd["xT"][c * NP:(c + 1) * NP, h * TH:(h + 1) * TH]))
        plain = [(self.c_ones_col[0:1, :], "ones_col"),
                 (self.c_ones_col[32:33, :], "ones_col"),
                 (self.c_ones_col[64:65, :], "ones_col"),
                 (self.c_ones_col[96:97, :], "ones_col"), (self.c_identf[:], "ident"),
                 (self.c_conv[:], "conv_w_r"), (self.c_bdt[:], "b_dt_r"),
                 (self.c_dpar[:], "d_param_r"), (self.c_na[:], "na_r"),
                 (self.c_g1[:], "g1_r"), (self.c_be1[:], "be1_r"),
                 (self.c_g2[:], "g2_r"), (self.c_be2[:], "be2_r"),
                 (self.c_outb[:], "out_b_r"), (self.c_b2[:], "b2_r"),
                 (self.c_b1[:], "b1_r"), (self.c_qkvbr[:], "qkv_b_r"),
                 (self.c_e2[0:2, :], "e2")]
        for dst, nm in plain:
            self.dma("sync", [self.bD[nm]], [self.bConst],
                     lambda s, dst=dst, nm=nm: s.dma_start(out=dst, in_=dd[nm]))
        casts = [(self.c_identb[:], "ident"), (self.c_ones_colb[0:1, :], "ones_col"),
                 (self.c_ones_k[:, 0:1], "ones_k"),
                 (self.c_e8[0:E, :], "e8"), (self.c_rbb[0:1, :], "rb")]
        for dst, nm in casts:
            self.dma("gpsimd", [self.bD[nm]], [self.bConst],
                     lambda g, dst=dst, nm=nm: g.dma_start(out=dst, in_=dd[nm]))
        self.dma("gpsimd", [self.bD["wrT"]], [self.bConst],
                 lambda g: g.dma_start(
                     out=self.c_wrT.rearrange("p (c e) -> p c e", e=self.E),
                     in_=dd["wrT"].rearrange("(c p) e -> p c e", p=NP)))
        for v, i in self.ck_map.items():
            self.op("vector", [], [self.bConst],
                    lambda vv, v=v, i=i: vv.memset(self.c_k[:, i:i + 1], v))
        self.op("scalar", [self.bConst], [self.bConst],
                lambda a: a.activation(out=self.c_na[:], in_=self.c_na[:], func=AF.Exp))
        self.op("vector", [self.bConst], [self.bConst],
                lambda v: v.tensor_scalar_mul(out=self.c_na[:], in0=self.c_na[:], scalar1=-1.0))
        self.op("vector", [self.bConst], [self.bConst],
                lambda v: v.tensor_scalar_mul(out=self.c_nb[:], in0=self.c_na[:], scalar1=1e-5))

    def emit_ln_rows(self, t0g, t1g, *, fused_attnb=False, eps=1e-5):
        DC, TQ, T = self.DC, self.TQ, self.T
        W = t1g - t0g
        ntile = W // NP
        NRC = 12
        self.A.mark()
        bxq, bST, bRM = Buf("ln_xq"), Buf("ln_st"), Buf("ln_rm")
        h0 = t0g // self.TH
        nhalf = (t1g - t0g + self.TH - 1) // self.TH
        xq = self.A.bf16(DC * W, bxq)
        stm = self.A.f32(2 * ntile, bST)
        rmm = self.A.f32(NRC * ntile, bRM)

        def xb_v(c, a, b):
            return self.xnb(c, t0g + a, t0g + b)
        for c in range(DC):
            for hh in range(nhalf):
                w0 = hh * self.TH
                w1 = min(W, (hh + 1) * self.TH)
                self.op("vector", self.bxr(c, t0g + w0, t0g + w1), [self.bXN[c][h0 + hh]],
                        lambda v, c=c, w0=w0, w1=w1: v.tensor_copy(
                            out=self.xnb(c, t0g + w0, t0g + w1),
                            in_=self.xr(c, t0g + w0, t0g + w1)))
            self.op("scalar", self.bxr(c, t0g, t1g), [bxq],
                    lambda a, c=c: a.activation(out=xq[:, c * W:(c + 1) * W],
                                                in_=self.xr(c, t0g, t1g), func=AF.Square))
        for j in range(ntile):
            kb = j % 2

            def mm(t, j=j, kb=kb):
                for c in range(DC):
                    t.matmul(out=self.pb[kb][:, 0:1],
                             lhsT=self.xnb(c, t0g + j * NP, t0g + (j + 1) * NP),
                             rhs=self.c_ones_k[:, 0:1],
                             start=(c == 0), stop=False)
                for c in range(DC):
                    ii = t.matmul(out=self.pb[kb][:, 1:2],
                                  lhsT=xq[:, c * W + j * NP:c * W + (j + 1) * NP],
                                  rhs=self.c_ones_k[:, 0:1],
                                  start=False, stop=(c == DC - 1))
                return ii
            xbb = [self.bXN[c][h0 + (j * NP) // self.TH] for c in range(DC)]
            self.op("tensor", xbb + [bxq, self.bConst], [self.bBank[kb]], mm)
            self.op("vector", [self.bBank[kb]], [bST],
                    lambda v, j=j, kb=kb: v.tensor_copy(out=stm[:, 2 * j:2 * j + 2],
                                                        in_=self.pb[kb][:, 0:2]))
        D = self.D

        def sl(k):
            return rmm.rearrange("p (n k) -> p n k", k=NRC)[:, :, k]
        s_mu, s_ms, s_var, s_tmp, s_a, s_b = (sl(k) for k in range(6))
        self.op("vector", [bST], [bRM],
                lambda v: v.tensor_scalar_mul(
                    out=s_mu, in0=stm.rearrange("p (n k) -> p n k", k=2)[:, :, 0],
                    scalar1=1.0 / D))
        self.op("vector", [bST], [bRM],
                lambda v: v.tensor_scalar_mul(
                    out=s_ms, in0=stm.rearrange("p (n k) -> p n k", k=2)[:, :, 1],
                    scalar1=1.0 / D))
        self.op("vector", [bRM], [bRM],
                lambda v: v.tensor_tensor(out=s_tmp, in0=s_mu, in1=s_mu, op=ALU.mult))
        self.op("vector", [bRM], [bRM],
                lambda v: v.tensor_tensor(out=s_var, in0=s_ms, in1=s_tmp, op=ALU.subtract))
        self.op("scalar", [bRM, self.bConst], [bRM],
                lambda a: a.activation(out=s_tmp, in_=s_var, func=AF.Ln, bias=self.ck(eps)))
        self.op("scalar", [bRM, self.bConst], [bRM],
                lambda a: a.activation(out=s_a, in_=s_tmp, func=AF.Exp, scale=self.ck(-0.5)))
        self.op("vector", [bRM], [bRM],
                lambda v: v.scalar_tensor_tensor(out=s_b, in0=s_mu, scalar=-1.0,
                                                 in1=s_a, op0=ALU.mult, op1=ALU.mult))
        npairs = 1
        if fused_attnb:
            s_a2, s_b2, s_s1 = sl(6), sl(7), sl(8)
            self.op("vector", [bRM], [bRM],
                    lambda v: v.tensor_tensor(out=s_s1, in0=s_var, in1=s_a, op=ALU.mult))
            self.op("vector", [bRM], [bRM],
                    lambda v: v.tensor_tensor(out=s_s1, in0=s_s1, in1=s_a, op=ALU.mult))
            self.op("scalar", [bRM, self.bConst], [bRM],
                    lambda a: a.activation(out=s_s1, in_=s_s1, func=AF.Ln, bias=self.ck(1e-5)))
            self.op("scalar", [bRM, self.bConst], [bRM],
                    lambda a: a.activation(out=s_s1, in_=s_s1, func=AF.Exp, scale=self.ck(-0.5)))
            self.op("vector", [bRM], [bRM],
                    lambda v: v.tensor_tensor(out=s_a2, in0=s_a, in1=s_s1, op=ALU.mult))
            self.op("vector", [bRM], [bRM],
                    lambda v: v.tensor_tensor(out=s_b2, in0=s_b, in1=s_s1, op=ALU.mult))
            npairs = 2
        for p_ in range(npairs):
            base = 4 if p_ == 0 else 6
            coff = 0 if p_ == 0 else T
            for j0 in range(0, ntile, 4):
                jn = min(4, ntile - j0)

                def trp(t, j0=j0, jn=jn, base=base):
                    for u in range(jn):
                        ii = t.transpose(
                            out=self.pb[2][0:2, u * NP:(u + 1) * NP],
                            in_=rmm[:, (j0 + u) * NRC + base:(j0 + u) * NRC + base + 2],
                            identity=self.c_identf[:])
                    return ii
                self.op("tensor", [bRM, self.bConst], [self.bBank[2]], trp)
                self.op("scalar", [self.bBank[2]], [self.bRows],
                        lambda a, j0=j0, jn=jn, coff=coff: a.copy(
                            out=self.rows2[0:2, coff + t0g + j0 * NP:coff + t0g + (j0 + jn) * NP],
                            in_=self.pb[2][0:2, 0:jn * NP]))
        self.A.release()

    def emit_apply_rows(self, coff, t0g, t1g, src_fn, src_bufs_fn, dst_fn, dst_bufs_fn,
                        tmp, btmp, gcol=None, becol=None, batched=None):
        DC, TQ, TH = self.DC, self.TQ, self.TH
        ntile = (t1g - t0g) // TQ
        for i in range(ntile):
            t0, t1 = t0g + i * TQ, t0g + (i + 1) * TQ
            h = t0 // TH
            lt0, lt1 = t0 - h * TH, t1 - h * TH
            kA, kB = (4, 5) if i % 2 == 0 else (6, 3)

            def mm(t, t0=t0, t1=t1, kA=kA, kB=kB, coff=coff):
                t.matmul(out=self.pb[kA][:, 0:TQ], lhsT=self.c_e2[0:2, 0:NP],
                         rhs=self.rows2[0:2, coff + t0:coff + t1], start=True, stop=True)
                return t.matmul(out=self.pb[kB][:, 0:TQ], lhsT=self.c_e2[0:2, NP:2 * NP],
                                rhs=self.rows2[0:2, coff + t0:coff + t1], start=True, stop=True)
            self.op("tensor", [self.bRows, self.bConst], [self.bBank[kA], self.bBank[kB]], mm)
            srcb = [b for cc in range(DC) for b in src_bufs_fn(cc, t0, t1)]
            dstb = [b for cc in range(DC) for b in dst_bufs_fn(cc, t0, t1)]
            tmp3 = tmp.rearrange("p (c t) -> p c t", t=TQ)
            bcA = self.pb[kA][:, 0:TQ].unsqueeze(1).broadcast_to([NP, DC, TQ])
            bcB = self.pb[kB][:, 0:TQ].unsqueeze(1).broadcast_to([NP, DC, TQ])
            if batched is not None:
                src3 = batched[0](h, lt0, lt1)
                dst3 = batched[1](h, lt0, lt1) if gcol is None else tmp3
                self.op("vector", srcb + [self.bBank[kA]], [btmp],
                        lambda v, src3=src3, bcA=bcA: v.tensor_tensor(
                            out=tmp3, in0=src3, in1=bcA, op=ALU.mult))
                self.op("vector", [btmp, self.bBank[kB]],
                        (dstb if gcol is None else [btmp]),
                        lambda v, dst3=dst3, bcB=bcB: v.tensor_tensor(
                            out=dst3, in0=tmp3, in1=bcB, op=ALU.add))
                if gcol is not None:
                    for c in range(DC):
                        self.op("vector", [btmp, self.bConst], dst_bufs_fn(c, t0, t1),
                                lambda v, c=c, t0=t0, t1=t1: v.tensor_scalar(
                                    out=dst_fn(c, t0, t1),
                                    in0=tmp[:, c * TQ:(c + 1) * TQ],
                                    scalar1=gcol[:, c:c + 1], scalar2=becol[:, c:c + 1],
                                    op0=ALU.mult, op1=ALU.add))
                continue
            for c in range(DC):
                self.op("vector", src_bufs_fn(c, t0, t1) + [self.bBank[kA]], [btmp],
                        lambda v, c=c, t0=t0, t1=t1, kA=kA: v.tensor_tensor(
                            out=tmp[:, (c % 2) * TQ:(c % 2) * TQ + TQ],
                            in0=src_fn(c, t0, t1), in1=self.pb[kA][:, 0:TQ], op=ALU.mult))
                if gcol is None:
                    self.op("vector", [btmp, self.bBank[kB]], dst_bufs_fn(c, t0, t1),
                            lambda v, c=c, t0=t0, t1=t1, kB=kB: v.tensor_tensor(
                                out=dst_fn(c, t0, t1),
                                in0=tmp[:, (c % 2) * TQ:(c % 2) * TQ + TQ],
                                in1=self.pb[kB][:, 0:TQ], op=ALU.add))
                else:
                    self.op("vector", [btmp, self.bBank[kB]], [btmp],
                            lambda v, c=c, kB=kB: v.tensor_tensor(
                                out=tmp[:, (c % 2) * TQ:(c % 2) * TQ + TQ],
                                in0=tmp[:, (c % 2) * TQ:(c % 2) * TQ + TQ],
                                in1=self.pb[kB][:, 0:TQ], op=ALU.add))
                    self.op("vector", [btmp, self.bConst], dst_bufs_fn(c, t0, t1),
                            lambda v, c=c, t0=t0, t1=t1: v.tensor_scalar(
                                out=dst_fn(c, t0, t1),
                                in0=tmp[:, (c % 2) * TQ:(c % 2) * TQ + TQ],
                                scalar1=gcol[:, c:c + 1], scalar2=becol[:, c:c + 1],
                                op0=ALU.mult, op1=ALU.add))

    def ln_into_xnb(self, t0g, t1g, eps=1e-5):
        self.emit_ln_rows(t0g, t1g, eps=eps)
        self.A.mark()
        btmp = Buf("ln_tmp")
        tmp = self.A.f32(self.DC * self.TQ, btmp)
        self.emit_apply_rows(0, t0g, t1g, self.xr, self.bxr, self.xnb,
                             lambda c, a, b: self.bxn(c, a, b), tmp, btmp,
                             batched=(self.xr3, self.xnb3))
        self.A.release()

    def phase_mamba(self):
        dd = self.dd
        D, T, DC, TQ, TC, NST, R, TH = self.D, self.T, self.DC, self.TQ, self.TC, self.NST, self.R, self.TH
        self.ln_into_xnb(0, TH)
        self.ln_into_xnb(TH, T)

        for c in range(DC):
            for h in range(2):
                self.dma("sync", [self.bXR[c][h]], [self.bD["scr_x"]], batch="mamxsp",
                         fn=lambda s, c=c, h=h: s.dma_start(
                             out=dd["scr_x"][c * NP:(c + 1) * NP, h * TH:(h + 1) * TH],
                             in_=self.xr(c, h * TH, (h + 1) * TH)))
        self.A.mark()
        SPL = NP
        bWx, bXPS, bWdt, bWc = (Buf(x) for x in ("w_x", "xps", "w_dt", "w_c"))
        bDtp = bBp = bCp = bXPS
        bT1, bT2 = Buf("t1"), Buf("t2")
        t1b = self.A.f32(T, bT1)
        t2b = self.A.f32(T, bT2)
        xps = self.A.bf16(T, bXPS)
        dtp_bf = xps
        bp_bf = xps
        cp_bf = xps
        w_x_buf = self.A.bf16(DC * SPL, bWx)
        w_dt_buf = self.A.bf16(DC * NP, bWdt)
        w_c_buf = self.A.bf16(DC * NP, bWc)
        for cin in range(DC):
            self.dma("gpsimd", [self.bD["w_xT"]], [bWx], batch="wx",
                     fn=lambda g, cin=cin: g.dma_start(
                         out=w_x_buf[:, cin * SPL:(cin + 1) * SPL],
                         in_=dd["w_xT"][cin * NP:(cin + 1) * NP, :]))
        self.dma("gpsimd", [self.bD["w_dtT"]], [bWdt], batch="wx",
                 fn=lambda g: g.dma_start(out=w_dt_buf[64:64 + R, :], in_=dd["w_dtT"]))
        self.dma("gpsimd", [self.bD["w_cT"]], [bWc], batch="wx",
                 fn=lambda g: g.dma_start(out=w_c_buf[32:32 + NST, :], in_=dd["w_cT"]))

        self.A.mark()
        OCH = min(4, 2 * DC)
        n_outg = (2 * DC) // OCH
        bWin2 = [[Buf(f"w_in{p}_{c}") for c in range(DC)] for p in range(2)]
        bXC2 = [Buf("xc0"), Buf("xc1")]
        bXSs = [Buf("xs_st0"), Buf("xs_st1")]
        bZs = [Buf("z_st0"), Buf("z_st1")]
        w_in_buf2 = [self.A.bf16(DC * OCH * NP, bWin2[0]),
                     self.A.bf16(DC * OCH * NP, bWin2[1])]
        xc_buf2 = [self.A.f32(T, bXC2[0]), self.A.f32(T, bXC2[1])]
        xs_stage = [self.A.f32(T, bXSs[0]), self.A.f32(T, bXSs[1])]
        z_stage = [self.A.f32(TQ, bZs[0]), self.A.f32(TQ, bZs[1])]
        OW = OCH * NP

        def load_win(g):
            gp = g % 2
            for cin in range(DC):
                self.dma("gpsimd", [self.bD["w_inT"]], [bWin2[gp][cin]], batch=("w_in", g),
                         fn=lambda gg, cin=cin, g=g, gp=gp: gg.dma_start(
                             out=w_in_buf2[gp][:, cin * OW:(cin + 1) * OW],
                             in_=dd["w_inT"][cin * NP:(cin + 1) * NP, g * OW:(g + 1) * OW]))
        load_win(0)
        for gidx in range(n_outg):
            gp = gidx % 2
            w_in_buf = w_in_buf2[gp]
            bWin = bWin2[gp]
            if gidx + 1 < n_outg:
                load_win(gidx + 1)
            for oc in range(OCH):
                cout = gidx * OCH + oc
                is_z = cout >= DC
                for i in range(TC):
                    t0, t1 = i * TQ, (i + 1) * TQ
                    bk = (oc * TC + i) % 3

                    def mm(t, oc=oc, t0=t0, t1=t1, bk=bk, w_in_buf=w_in_buf):
                        for cin in range(DC):
                            ii = t.matmul(out=self.pb[bk][:, 0:TQ],
                                          lhsT=w_in_buf[:, cin * OW + oc * NP:cin * OW + (oc + 1) * NP],
                                          rhs=self.xnb(cin, t0, t1),
                                          start=(cin == 0), stop=(cin == DC - 1))
                        return ii
                    rbufs = list(bWin) + [b for cc in range(DC) for b in self.bxn(cc, t0, t1)]
                    self.op("tensor", rbufs, [self.bBank[bk]], mm)
                    if not is_z:
                        self.op("vector", [self.bBank[bk]], [bXC2[cout % 2]],
                                lambda v, t0=t0, t1=t1, bk=bk, cout=cout: v.tensor_copy(
                                    out=xc_buf2[cout % 2][:, t0:t1], in_=self.pb[bk][:, 0:TQ]))
                    else:
                        zc = cout - DC
                        st = i % 2
                        self.op("scalar", [self.bBank[bk]], [bZs[st]],
                                lambda a, bk=bk, st=st: a.copy(out=z_stage[st][:, 0:TQ],
                                                               in_=self.pb[bk][:, 0:TQ]))
                        self.dma("sync", [bZs[st]], [self.bD["scr_z"]],
                                 fn=lambda s, zc=zc, i=i, st=st: s.dma_start(
                                     out=dd["scr_z"][zc * NP:(zc + 1) * NP, i * TQ:(i + 1) * TQ],
                                     in_=z_stage[st][:, 0:TQ]))
                if not is_z:
                    c = cout
                    stg = xs_stage[c % 2]
                    bst = bXSs[c % 2]
                    xcb = xc_buf2[c % 2]
                    bxc = bXC2[c % 2]
                    self.op("vector", [bxc, self.bConst], [bst],
                            lambda v, c=c, stg=stg, xcb=xcb: v.tensor_scalar_mul(
                                out=stg[:], in0=xcb[:],
                                scalar1=self.c_conv[:, c * 3 + 1:c * 3 + 2]))
                    self.op("vector", [bxc, self.bConst, bst], [bst],
                            lambda v, c=c, stg=stg, xcb=xcb: v.scalar_tensor_tensor(
                                out=stg[:, 1:T], in0=xcb[:, 0:T - 1],
                                scalar=self.c_conv[:, c * 3 + 0:c * 3 + 1],
                                in1=stg[:, 1:T], op0=ALU.mult, op1=ALU.add))
                    self.op("vector", [bxc, self.bConst, bst], [bst],
                            lambda v, c=c, stg=stg, xcb=xcb: v.scalar_tensor_tensor(
                                out=stg[:, 0:T - 1], in0=xcb[:, 1:T],
                                scalar=self.c_conv[:, c * 3 + 2:c * 3 + 3],
                                in1=stg[:, 0:T - 1], op0=ALU.mult, op1=ALU.add))
                    self.op("scalar", [bst], [bst],
                            lambda a, stg=stg: a.activation(out=stg[:], in_=stg[:], func=AF.Silu))
                    self.dma("sync", [bst], [self.bD["scr_xs"]],
                             fn=lambda s, c=c, stg=stg: s.dma_start(
                                 out=dd["scr_xs"][c * NP:(c + 1) * NP, :], in_=stg[:]))
                    self.dma("gpsimd", [bst], [self.bD["scr_xs_bf"]],
                             fn=lambda g, c=c, stg=stg: g.dma_start(
                                 out=dd["scr_xs_bf"][c * NP:(c + 1) * NP, :], in_=stg[:]))
        self.A.release()

        self.A.mark()
        bXBT = [Buf("xbt0"), Buf("xbt1")]
        xbt = [self.A.bf16(T, bXBT[0]), self.A.bf16(T, bXBT[1])]
        bkA = list(range(TC))
        for cin in range(DC):
            pp = cin % 2
            self.dma("sync", [self.bD["scr_xs_bf"]], [bXBT[pp]],
                     fn=lambda s, cin=cin, pp=pp: s.dma_start(
                         out=xbt[pp][:], in_=dd["scr_xs_bf"][cin * NP:(cin + 1) * NP, :]))

            def mmx(t, cin=cin, pp=pp):
                for i in range(TC):
                    t0, t1 = i * TQ, (i + 1) * TQ
                    ii = t.matmul(out=self.pb[bkA[i]][0:SPL, 0:TQ],
                                  lhsT=w_x_buf[:, cin * SPL:(cin + 1) * SPL],
                                  rhs=xbt[pp][:, t0:t1],
                                  start=(cin == 0), stop=(cin == DC - 1))
                return ii
            self.op("tensor", [bWx, bXBT[pp]],
                    [self.bBank[k] for k in bkA], mmx)
        for i in range(TC):
            t0, t1 = i * TQ, (i + 1) * TQ

            self.op("vector", [self.bBank[bkA[i]]], [bXPS],
                    lambda v, i=i, t0=t0, t1=t1: v.tensor_copy(
                        out=xps[:, t0:t1], in_=self.pb[bkA[i]][:, 0:TQ]))
        self.dma("sync", [bXPS], [self.bD["scr_b"]], batch="scrb",
                 fn=lambda s: s.dma_start(out=dd["scr_b"], in_=xps[0:NST, 0:T]))
        self.A.release()

        self.A.mark()
        _bdt = Buf("dt0")
        bDt = [_bdt, _bdt]
        bU, bXsl = Buf("u"), Buf("xsl")
        bS2 = [Buf("s2_0"), Buf("s2_1")]
        bBcbH = [[Buf("bcb00"), Buf("bcb01")], [Buf("bcb10"), Buf("bcb11")]]
        bA2 = [Buf("A2_0"), Buf("A2_1")]
        bBx2 = [Buf("Bx2_0"), Buf("Bx2_1")]
        bcb = [self.A.bf16(2 * T, bBcbH[0]), self.A.bf16(2 * T, bBcbH[1])]
        _dtb = self.A.f32(T, bDt[0])
        dtb2 = [_dtb, _dtb]
        ub = self.A.f32(T, bU)
        xsl = self.A.f32(T, bXsl)
        s2a = self.A.f32(2 * T, bS2[0])
        t1_off = t1b.offset
        t2_off = t2b.offset
        assert t2_off == t1_off + T, (t1_off, t2_off)
        s2b = self.sb[:, t1_off:t1_off + 2 * T]
        self.alias_seed([bS2[1]], [bT1, bT2])
        s22 = [s2a, s2b]
        allxrb = [self.bXR[cc][hh] for cc in range(DC) for hh in range(2)]
        if DC * T >= 8 * T:
            A2 = [self.XR[:, 0:2 * T], self.XR[:, 2 * T:4 * T]]
            Bx2 = [self.XR[:, 4 * T:6 * T], self.XR[:, 6 * T:8 * T]]
            self.alias_seed(bA2 + bBx2, allxrb)
            scan_overlay = True
        else:
            A2 = [self.A.f32(2 * T, bA2[0]), self.A.f32(2 * T, bA2[1])]
            Bx2 = [self.A.f32(2 * T, bBx2[0]), self.A.f32(2 * T, bBx2[1])]
            scan_overlay = False

        def ybf_v(c, a, b):
            return self.XNB[:, c * T + a:c * T + b]
        bYBF = Buf("ybf")
        allxn = [b for cc in range(DC) for hh in range(2) for b in [self.bXN[cc][hh]]]
        NPAIR = NST // 2

        for c in range(DC):
            dtb, bDtc = dtb2[c % 2], bDt[c % 2]
            self.dma("sync", [self.bD["scr_xs"], bU, bT1], [bXsl],
                     fn=lambda s, c=c: s.dma_start(out=xsl[:],
                                                   in_=dd["scr_xs"][c * NP:(c + 1) * NP, :]))
            for i in range(TC):
                t0, t1 = i * TQ, (i + 1) * TQ
                self.op("tensor", [bWdt, bXPS], [self.bBank[6]],
                        lambda t, c=c, t0=t0, t1=t1: t.matmul(
                            out=self.pb[6][:, 0:TQ],
                            lhsT=w_dt_buf[64:64 + R, c * NP:(c + 1) * NP],
                            rhs=xps[64:64 + R, t0:t1], start=True, stop=True))
                self.op("vector", [self.bBank[6], self.bConst], [bDtc],
                        lambda v, c=c, t0=t0, t1=t1, dtb=dtb: v.tensor_scalar(
                            out=dtb[:, t0:t1], in0=self.pb[6][:, 0:TQ],
                            scalar1=self.c_bdt[:, c:c + 1], scalar2=None, op0=ALU.add))
            self.op("vector", [bDtc], [bT1],
                    lambda v, dtb=dtb: v.scalar_tensor_tensor(
                        out=t1b[:], in0=dtb[:], scalar=-1.0,
                        in1=dtb[:], op0=ALU.mult, op1=ALU.max))
            self.op("scalar", [bT1, self.bConst], [bT2],
                    lambda a: a.activation(out=t2b[:], in_=t1b[:], func=AF.Exp, scale=self.ck(-1.0)))
            self.op("scalar", [bT2, self.bConst], [bT1],
                    lambda a: a.activation(out=t1b[:], in_=t2b[:], func=AF.Ln, bias=self.ck(1.0)))
            self.op("vector", [bT1, bDtc], [bDtc],
                    lambda v, dtb=dtb: v.scalar_tensor_tensor(
                        out=dtb[:], in0=dtb[:], scalar=0.0,
                        in1=t1b[:], op0=ALU.max, op1=ALU.add))
            self.op("vector", [bDtc, bXsl], [bU],
                    lambda v, dtb=dtb: v.scalar_tensor_tensor(
                        out=ub[:], in0=dtb[:], scalar=1e-5,
                        in1=xsl[:], op0=ALU.add, op1=ALU.mult))
            self.op("vector", [bDtc], [bDtc],
                    lambda v, dtb=dtb: v.memset(dtb[:, 0:1], 1e30))
            def prep(j, c=c, dtb=dtb, bDtc=bDtc):
                pj = j % 2
                Ab, Bxb, bAb, bBxb = A2[pj], Bx2[pj], bA2[pj], bBx2[pj]
                bce = bcb[pj]
                for half_n in range(2):
                    n = 2 * j + half_n
                    self.op("scalar", [bDtc, self.bConst], [bAb],
                            lambda a, c=c, n=n, half_n=half_n, Ab=Ab, dtb=dtb: a.activation(
                                out=Ab[:, half_n * T:(half_n + 1) * T], in_=dtb[:], func=AF.Exp,
                                scale=self.c_na[:, c * NST + n:c * NST + n + 1],
                                bias=self.c_nb[:, c * NST + n:c * NST + n + 1]))
                    self.dma("sync", [self.bD["scr_b"]], [bBcbH[pj][half_n]], batch=("bbc", c, j),
                             fn=lambda s, n=n, half_n=half_n, bce=bce: s.dma_start(
                                 out=bce[:, half_n * T:(half_n + 1) * T],
                                 in_=dd["scr_b"][n:n + 1, :].broadcast_to([NP, T])))
                    self.op("vector", [bBcbH[pj][half_n], bU], [bBxb],
                            lambda v, half_n=half_n, bce=bce, Bxb=Bxb:
                            v.tensor_tensor(
                                out=Bxb[:, half_n * T:(half_n + 1) * T],
                                in0=ub[:],
                                in1=bce[:, half_n * T:(half_n + 1) * T], op=ALU.mult))

            prep(0)
            for j in range(NPAIR):
                pj = j % 2
                Ab, Bxb, bAb, bBxb = A2[pj], Bx2[pj], bA2[pj], bBx2[pj]
                s2, bS = s22[pj], bS2[pj]
                alias_tt = [bT1, bT2] if pj == 1 else []
                self.op("vector", [bAb, bBxb], [bS] + alias_tt,
                        lambda v, Ab=Ab, Bxb=Bxb, s2=s2: v.tensor_tensor_scan(
                            out=s2[:], data0=Ab[:], data1=Bxb[:],
                            initial=0.0, op0=ALU.mult, op1=ALU.add))
                if j + 1 < NPAIR:
                    prep(j + 1)

                def accmm(t, j=j, s2=s2):
                    for hn in range(2):
                        for i in range(TC):
                            ii = t.matmul(out=self.pb[i][:, 0:TQ], lhsT=self.c_identf[:],
                                          rhs=s2[:, hn * T + i * TQ:hn * T + (i + 1) * TQ],
                                          start=(j == 0 and hn == 0),
                                          stop=(j == NPAIR - 1 and hn == 1))
                    return ii
                self.op("tensor", [bS, self.bConst] + alias_tt,
                        [self.bBank[i] for i in range(TC)], accmm)
            for i in range(TC):
                t0, t1 = i * TQ, (i + 1) * TQ
                self.op("tensor", [bWc, bXPS], [self.bBank[6]],
                        lambda t, c=c, t0=t0, t1=t1: t.matmul(
                            out=self.pb[6][:, 0:TQ],
                            lhsT=w_c_buf[32:32 + NST, c * NP:(c + 1) * NP],
                            rhs=xps[32:32 + NST, t0:t1], start=True, stop=True))
                self.op("scalar", [self.bBank[6]], [bT2],
                        lambda a, t0=t0, t1=t1: a.copy(out=t2b[:, t0:t1],
                                                       in_=self.pb[6][:, 0:TQ]))
                self.op("vector", [bT2, self.bBank[i]], [bT1],
                        lambda v, i=i, t0=t0, t1=t1: v.tensor_tensor(
                            out=t1b[:, t0:t1], in0=t2b[:, t0:t1],
                            in1=self.pb[i][:, 0:TQ], op=ALU.mult))
            self.op("vector", [bT1, bXsl, self.bConst], [bYBF] + (allxn if c == 0 else []),
                    lambda v, c=c: v.scalar_tensor_tensor(
                        out=ybf_v(c, 0, T), in0=xsl[:],
                        scalar=self.c_dpar[:, c:c + 1], in1=t1b[:],
                        op0=ALU.mult, op1=ALU.add))
        self.A.release()

        for c in range(DC):
            for h in range(2):
                self.dma("sync", [self.bD["scr_x"]],
                         [self.bXR[c][h]] + (bA2 + bBx2 if (scan_overlay and c == 0 and h == 0) else []),
                         batch="mamxrl",
                         fn=lambda s, c=c, h=h: s.dma_start(
                             out=self.xr(c, h * TH, (h + 1) * TH),
                             in_=dd["scr_x"][c * NP:(c + 1) * NP, h * TH:(h + 1) * TH]))

        self.A.mark()
        bWo = [Buf(f"w_out{c}") for c in range(DC)]
        bZl = [Buf("zld0"), Buf("zld1")]
        bSzt = [Buf("szt0"), Buf("szt1")]
        bWT = Buf("wtmp")
        wtmp = self.A.f32(T, bWT)
        w_out_buf = self.A.bf16(DC * DC * NP, bWo)
        zld = [self.A.f32(TQ, bZl[0]), self.A.f32(TQ, bZl[1])]
        szt = [self.A.f32(TQ, bSzt[0]), self.A.f32(TQ, bSzt[1])]
        for cin in range(DC):
            self.dma("gpsimd", [self.bD["w_outT"]], [bWo[cin]], batch="w_out",
                     fn=lambda g, cin=cin: g.dma_start(
                         out=w_out_buf[:, cin * DC * NP:(cin + 1) * DC * NP],
                         in_=dd["w_outT"][cin * NP:(cin + 1) * NP, :]))
        for cout in range(DC):
            for i in range(TC):
                t0, t1 = i * TQ, (i + 1) * TQ
                st = i % 2
                kb = 4 + (i % 2)
                self.dma("sync", [self.bD["scr_z"]], [bZl[st]],
                         fn=lambda s, cout=cout, t0=t0, t1=t1, st=st: s.dma_start(
                             out=zld[st][:, 0:TQ],
                             in_=dd["scr_z"][cout * NP:(cout + 1) * NP, t0:t1]))
                self.op("scalar", [bZl[st]], [bSzt[st]],
                        lambda a, st=st: a.activation(out=szt[st][:, 0:TQ], in_=zld[st][:, 0:TQ],
                                                      func=AF.Silu))

                def mm(t, cout=cout, t0=t0, t1=t1, kb=kb):
                    for cin in range(DC):
                        ii = t.matmul(out=self.pb[kb][:, 0:TQ],
                                      lhsT=w_out_buf[:, cin * DC * NP + cout * NP:
                                                     cin * DC * NP + (cout + 1) * NP],
                                      rhs=ybf_v(cin, t0, t1),
                                      start=(cin == 0), stop=(cin == DC - 1))
                    return ii
                self.op("tensor", list(bWo) + [bYBF], [self.bBank[kb]], mm)
                self.op("vector", [self.bBank[kb], bSzt[st]], [bWT],
                        lambda v, t0=t0, t1=t1, kb=kb, st=st: v.tensor_tensor(
                            out=wtmp[:, t0:t1], in0=szt[st][:, 0:TQ],
                            in1=self.pb[kb][:, 0:TQ], op=ALU.mult))
                self.op("vector", [bWT] + self.bxr(cout, t0, t1), self.bxr(cout, t0, t1),
                        lambda v, cout=cout, t0=t0, t1=t1: v.tensor_tensor(
                            out=self.xr(cout, t0, t1), in0=self.xr(cout, t0, t1),
                            in1=wtmp[:, t0:t1], op=ALU.add))
        self.A.release()
        self.alias_seed([b for cc in range(DC) for hh in range(2)
                         for b in [self.bXN[cc][hh]]], [bYBF])
        self.A.release()

    def phase_moe(self):
        dd = self.dd
        D, T, DC, TQ, NST, TH, I, IC, E = self.D, self.T, self.DC, self.TQ, self.NST, self.TH, self.I, self.IC, self.E
        for h in range(2):
            t0h, t1h = h * TH, (h + 1) * TH
            oh = 1 - h
            for c in range(DC):
                self.dma("sync", [self.bXR[c][oh]], [self.bD["scr_x"]], batch=("mxsp", h),
                         fn=lambda s, c=c, oh=oh: s.dma_start(
                             out=dd["scr_x"][c * NP:(c + 1) * NP, oh * TH:(oh + 1) * TH],
                             in_=self.xr(c, oh * TH, (oh + 1) * TH)))
            self.A.mark()
            NQW0 = 4 if IC % 4 == 0 else 1
            IQ0 = I // NQW0
            bWgu = [[Buf(f"wguq{p}_{c}") for c in range(2 * DC)] for p in range(2)]
            wguq = [self.A.bf16(2 * DC * IQ0, bWgu[0]), self.A.bf16(2 * DC * IQ0, bWgu[1])]
            self.ln_into_xnb(t0h, t1h)
            bCWR, bL8, bMR, bL2, bMS, bSG, bCW = (Buf(x) for x in
                                                  ("cwr", "l8", "mr", "l2", "msk", "sg", "cwb"))
            CWR = self.A.bf16(TH, bCWR)
            l8 = self.A.f32(8, bL8)
            mr8 = self.A.f32(8, bMR)
            l2_ = self.A.f32(8, bL2)
            msk = self.A.f32(8, bMS)
            sg8 = self.A.f32(8, bSG)
            cwb_ = self.A.bf16(8, bCW)
            for j in range(self.TTH):
                jt0 = t0h + j * NP

                def mmr(t, jt0=jt0):
                    for cin in range(DC):
                        t.matmul(out=self.pb[0][:, 0:E],
                                 lhsT=self.xnb(cin, jt0, jt0 + NP),
                                 rhs=self.c_wrT[:, cin * E:(cin + 1) * E],
                                 start=(cin == 0), stop=False)
                    return t.matmul(out=self.pb[0][:, 0:E], lhsT=self.c_ones_colb[0:1, :],
                                    rhs=self.c_rbb[0:1, 0:E], start=False, stop=True)
                self.op("tensor", [self.bConst] + [b for cc in range(DC) for b in self.bxn(cc, jt0, jt0 + NP)],
                        [self.bBank[0]], mmr)
                self.op("vector", [self.bBank[0]], [bL8],
                        lambda v, l8=l8: v.tensor_copy(out=l8[:, 0:E], in_=self.pb[0][:, 0:E]))
                self.op("vector", [bL8], [bMR],
                        lambda v, l8=l8, mr8=mr8: v.max(out=mr8[:, 0:8], in_=l8[:, 0:E]))
                self.op("vector", [bMR], [bMR],
                        lambda v, mr8=mr8: v.memset(mr8[:, 2:8], -1e38))
                self.op("vector", [bMR, bL8], [bL2],
                        lambda v, l8=l8, mr8=mr8, l2_=l2_: v.match_replace(
                            out=l2_[:, 0:E], in_to_replace=mr8[:, 0:8],
                            in_values=l8[:, 0:E], imm_value=-1e30))
                self.op("vector", [bL2], [bMS],
                        lambda v, l2_=l2_, msk=msk: v.tensor_single_scalar(
                            out=msk[:, 0:E], in_=l2_[:, 0:E], scalar=-1e30, op=ALU.is_equal))
                self.op("scalar", [bL8], [bSG],
                        lambda a, l8=l8, sg8=sg8: a.activation(out=sg8[:, 0:E], in_=l8[:, 0:E],
                                                               func=AF.Sigmoid))
                self.op("vector", [bMS, bSG], [bCW],
                        lambda v, sg8=sg8, msk=msk, cwb_=cwb_: v.tensor_tensor(
                            out=cwb_[:, 0:E], in0=sg8[:, 0:E], in1=msk[:, 0:E], op=ALU.mult))
                self.op("tensor", [bCW, self.bConst], [self.bBankF],
                        lambda t, cwb_=cwb_: t.transpose(out=self.pbf[0:E, 0:NP], in_=cwb_[:, 0:E],
                                                         identity=self.c_identb[:]))
                self.op("scalar", [self.bBankF], [bCWR],
                        lambda a, j=j, CWR=CWR: a.copy(out=CWR[0:E, j * NP:(j + 1) * NP],
                                                       in_=self.pbf[0:E, 0:NP]))
            NQW = NQW0
            IQ = IQ0
            ICQ = IC // NQW
            bWd = [Buf(f"wd{ic}") for ic in range(IC)]
            bH = Buf("hbf")
            bSg = [Buf("sg0"), Buf("sg1")]
            hbf = self.A.bf16(IC * self.TCH * TQ, bH)
            sgb = [self.A.bf16(TQ, bSg[0]), self.A.bf16(TQ, bSg[1])]
            if 2 * DC * TH >= IC * D:
                wd_region = self.xr_half_region_bf16(oh)[:, 0:IC * D]
                wd_in_xr = True
                self.alias_seed(bWd, [self.bXR[c][oh] for c in range(DC)])
            else:
                wd_region = self.A.bf16(IC * D, bWd)
                wd_in_xr = False
            xne_region = self.xnb_half_region(oh)

            def xne(c, q0, q1, xne_region=xne_region):
                return xne_region[:, c * TH + q0:c * TH + q1]

            spill_bufs = [self.bXR[c][oh] for c in range(DC)]

            def load_wgu_q(e2, qq):
                pp = qq % 2
                nmg2 = "wg_sT" if e2 == 0 else "wgT"
                nmu2 = "wu_sT" if e2 == 0 else "wuT"
                eh2 = e2 - 1
                for cin in range(DC):
                    for mi, nm in ((0, nmg2), (1, nmu2)):
                        self.dma("gpsimd", [self.bD[nm]], [bWgu[pp][mi * DC + cin]],
                                 batch=("wgu", e2, qq),
                                 fn=lambda g, cin=cin, nm=nm, eh2=eh2, mi=mi, qq=qq, pp=pp:
                                 g.dma_start(
                                     out=wguq[pp][:, (mi * DC + cin) * IQ:
                                                  (mi * DC + cin + 1) * IQ],
                                     in_=(dd[nm][cin * NP:(cin + 1) * NP,
                                                 qq * IQ:(qq + 1) * IQ] if eh2 < 0
                                          else dd[nm][eh2, cin * NP:(cin + 1) * NP,
                                                      qq * IQ:(qq + 1) * IQ])))
            load_wgu_q(0, 0)
            for e in range(E + 1):
                nm_d = "wd_sT" if e == 0 else "wdT"
                eh = e - 1
                for ic in range(IC):
                    extra = (spill_bufs if (e == 0 and wd_in_xr) else [])
                    self.dma("gpsimd", [self.bD[nm_d]], [bWd[ic]] + extra, batch=("wd", e),
                             fn=lambda g, ic=ic, nm=nm_d, eh=eh, wd_region=wd_region: g.dma_start(
                                 out=wd_region[:, ic * D:(ic + 1) * D],
                                 in_=(dd[nm][ic * NP:(ic + 1) * NP, :] if eh < 0
                                      else dd[nm][eh, ic * NP:(ic + 1) * NP, :])))
                def emit_xne(er, xne=xne, t0h=t0h, oh=oh, CWR=CWR):
                    ehr = er - 1
                    for q in range(self.TCH):
                        q0 = q * TQ
                        self.op("tensor", [bCWR, self.bConst], [self.bBank[6]],
                                lambda t, ehr=ehr, q0=q0, CWR=CWR: t.matmul(
                                    out=self.pb[6][:, 0:TQ],
                                    lhsT=self.c_e8[0:E, ehr * NP:(ehr + 1) * NP],
                                    rhs=CWR[0:E, q0:q0 + TQ], start=True, stop=True))
                        for c in range(DC):
                            self.op("vector", [self.bBank[6]] + self.bxn(c, t0h + q0, t0h + q0 + TQ),
                                    [self.bXN[c][oh]],
                                    lambda v, c=c, q0=q0, xne=xne, t0h=t0h: v.tensor_tensor(
                                        out=xne(c, q0, q0 + TQ),
                                        in0=self.xnb(c, t0h + q0, t0h + q0 + TQ),
                                        in1=self.pb[6][:, 0:TQ], op=ALU.mult))
                if e == 1:
                    pass
                for qq in range(NQW):
                    pp = qq % 2
                    if qq + 1 < NQW:
                        load_wgu_q(e, qq + 1)
                    elif e + 1 <= E:
                        load_wgu_q(e + 1, 0)
                    for q in range(self.TCH):
                        q0 = q * TQ
                        for icl in range(ICQ):
                            ic = qq * ICQ + icl
                            kg, ku = ic % 2, 2 + ic % 2

                            def mmg(t, icl=icl, q0=q0, kg=kg, pp=pp, t0h=t0h):
                                for cin in range(DC):
                                    ii = t.matmul(
                                        out=self.pb[kg][:, 0:TQ],
                                        lhsT=wguq[pp][:, cin * IQ + icl * NP:
                                                      cin * IQ + (icl + 1) * NP],
                                        rhs=self.xnb(cin, t0h + q0, t0h + q0 + TQ),
                                        start=(cin == 0), stop=(cin == DC - 1))
                                return ii
                            self.op("tensor", bWgu[pp][0:DC] + [b for cc in range(DC)
                                    for b in self.bxn(cc, t0h + q0, t0h + q0 + TQ)],
                                    [self.bBank[kg]], mmg)

                            def mmu(t, icl=icl, q0=q0, ku=ku, e=e, xne=xne, pp=pp, t0h=t0h):
                                for cin in range(DC):
                                    if e == 0:
                                        rhs = self.xnb(cin, t0h + q0, t0h + q0 + TQ)
                                    else:
                                        rhs = xne(cin, q0, q0 + TQ)
                                    ii = t.matmul(
                                        out=self.pb[ku][:, 0:TQ],
                                        lhsT=wguq[pp][:, (DC + cin) * IQ + icl * NP:
                                                      (DC + cin) * IQ + (icl + 1) * NP],
                                        rhs=rhs, start=(cin == 0), stop=(cin == DC - 1))
                                return ii
                            ub_r = [b for cc in range(DC)
                                    for b in self.bxn(cc, t0h + q0, t0h + q0 + TQ)] \
                                if e == 0 else [self.bXN[cc][oh] for cc in range(DC)]
                            self.op("tensor", bWgu[pp][DC:2 * DC] + ub_r, [self.bBank[ku]], mmu)
                            self.op("scalar", [self.bBank[kg]], [bSg[ic % 2]],
                                    lambda a, kg=kg, ic=ic, sgb=sgb: a.activation(
                                        out=sgb[ic % 2][:, 0:TQ], in_=self.pb[kg][:, 0:TQ],
                                        func=AF.Silu))
                            self.op("vector", [self.bBank[ku], bSg[ic % 2]], [bH],
                                    lambda v, ic=ic, ku=ku, q=q, sgb=sgb, hbf=hbf: v.tensor_tensor(
                                        out=hbf[:, (ic * self.TCH + q) * TQ:
                                                 (ic * self.TCH + q + 1) * TQ],
                                        in0=sgb[ic % 2][:, 0:TQ],
                                        in1=self.pb[ku][:, 0:TQ], op=ALU.mult))
                if e + 1 <= E:
                    emit_xne(e + 1)
                for q in range(self.TCH):
                    q0 = q * TQ
                    for hc in range(DC):
                        kd = 4 + (hc % 2)

                        def mmd(t, hc=hc, kd=kd, q=q, wd_region=wd_region, hbf=hbf):
                            for ic in range(IC):
                                ii = t.matmul(out=self.pb[kd][:, 0:TQ],
                                              lhsT=wd_region[:, ic * D + hc * NP:
                                                             ic * D + (hc + 1) * NP],
                                              rhs=hbf[:, (ic * self.TCH + q) * TQ:
                                                       (ic * self.TCH + q + 1) * TQ],
                                              start=(ic == 0), stop=(ic == IC - 1))
                            return ii
                        self.op("tensor", list(bWd) + [bH], [self.bBank[kd]], mmd)
                        self.op("vector", [self.bBank[kd]] + self.bxr(hc, t0h + q0, t0h + q0 + TQ),
                                self.bxr(hc, t0h + q0, t0h + q0 + TQ),
                                lambda v, hc=hc, q0=q0, kd=kd, t0h=t0h: v.tensor_tensor(
                                    out=self.xr(hc, t0h + q0, t0h + q0 + TQ),
                                    in0=self.xr(hc, t0h + q0, t0h + q0 + TQ),
                                    in1=self.pb[kd][:, 0:TQ], op=ALU.add))
            for c in range(DC):
                self.dma("sync", [self.bD["scr_x"]],
                         [self.bXR[c][oh]] + (list(bWd) if wd_in_xr else []),
                         batch=("mxrl", h),
                         fn=lambda s, c=c, oh=oh: s.dma_start(
                             out=self.xr(c, oh * TH, (oh + 1) * TH),
                             in_=dd["scr_x"][c * NP:(c + 1) * NP, oh * TH:(oh + 1) * TH]))
            self.A.release()

    def xr_half_region_bf16(self, half):
        DC, TH = self.DC, self.TH
        v = self.XR[:, half * DC * TH:(half + 1) * DC * TH]
        return v.bitcast(BF16)

    def xr3(self, h, a, b):
        DC, TH = self.DC, self.TH
        return self.XR[:, h * DC * TH:(h + 1) * DC * TH].rearrange(
            "p (c t) -> p c t", t=TH)[:, :, a:b]

    def xnb3(self, h, a, b):
        DC, TH = self.DC, self.TH
        return self.XNB[:, h * DC * TH:(h + 1) * DC * TH].rearrange(
            "p (c t) -> p c t", t=TH)[:, :, a:b]

    def xnb_half_region(self, half):
        DC, TH = self.DC, self.TH
        return self.XNB[:, half * DC * TH:(half + 1) * DC * TH]

    def phase_attnb(self):
        dd = self.dd
        D, T, DC, TQ, TH, NH, FF, FC = self.D, self.T, self.DC, self.TQ, self.TH, self.NH, self.FF, self.FC
        self.emit_ln_rows(0, TH, fused_attnb=True)
        self.emit_ln_rows(TH, T, fused_attnb=True)
        for c in range(DC):
            for h in range(2):
                self.dma("sync", [self.bXR[c][h]], [self.bD["scr_x"]], batch="absp",
                         fn=lambda s, c=c, h=h: s.dma_start(
                             out=dd["scr_x"][c * NP:(c + 1) * NP, h * TH:(h + 1) * TH],
                             in_=self.xr(c, h * TH, (h + 1) * TH)))
        self.A.mark()
        btmp = Buf("ab_tmp")
        tmp = self.A.f32(DC * TQ, btmp)
        self.emit_apply_rows(T, 0, T, self.xr, self.bxr,
                             self.xnb, lambda c, a, b: self.bxn(c, a, b), tmp, btmp,
                             gcol=self.c_g1, becol=self.c_be1,
                             batched=(self.xr3, self.xnb3))
        self.A.release()

        self.A.mark()
        bQW = [Buf(f"qw{c}") for c in range(DC)]
        qw = self.A.bf16(DC * D, bQW)
        for cin in range(DC):
            self.dma("gpsimd", [self.bD["qkv_w"]], [bQW[cin]], batch="qw",
                     fn=lambda g, cin=cin: g.dma_start(
                         out=qw[:, cin * D:(cin + 1) * D],
                         in_=dd["qkv_w"][cin * NP:(cin + 1) * NP, 0:D]))
        bQF = Buf("qfb")
        bTQF, bTEQ = Buf("tqf"), Buf("teq")
        QFB = self.A.bf16(DC * TH, bQF)
        tqf = self.A.f32(2 * TQ, bTQF)
        teq = self.A.f32(2 * TQ, bTEQ)
        for h in range(2):
            t0h = h * TH
            for qc in range(DC):
                for q in range(self.TCH):
                    q0 = t0h + q * TQ
                    kb = q % 2

                    def mmq(t, qc=qc, q0=q0, kb=kb):
                        for cin in range(DC):
                            ii = t.matmul(out=self.pb[kb][:, 0:TQ],
                                          lhsT=qw[:, cin * D + qc * NP:cin * D + (qc + 1) * NP],
                                          rhs=self.xnb(cin, q0, q0 + TQ),
                                          start=(cin == 0), stop=(cin == DC - 1))
                        return ii
                    self.op("tensor", list(bQW) + [b for cc in range(DC) for b in self.bxn(cc, q0, q0 + TQ)],
                            [self.bBank[kb]], mmq)
                    sl = slice((q % 2) * TQ, (q % 2) * TQ + TQ)
                    self.op("vector", [self.bBank[kb], self.bConst], [bTQF],
                            lambda v, qc=qc, kb=kb, sl=sl: v.tensor_scalar(
                                out=tqf[:, sl], in0=self.pb[kb][:, 0:TQ],
                                scalar1=self.c_qkvbr[:, qc:qc + 1], scalar2=0.0,
                                op0=ALU.add, op1=ALU.min))
                    self.op("scalar", [bTQF], [bTEQ],
                            lambda a, sl=sl: a.activation(out=teq[:, sl], in_=tqf[:, sl],
                                                          func=AF.Exp))
                    self.op("vector", [self.bBank[kb], self.bConst], [bTQF],
                            lambda v, qc=qc, kb=kb, sl=sl: v.tensor_scalar(
                                out=tqf[:, sl], in0=self.pb[kb][:, 0:TQ],
                                scalar1=self.c_qkvbr[:, qc:qc + 1], scalar2=0.0,
                                op0=ALU.add, op1=ALU.max))
                    self.op("vector", [bTQF, bTEQ], [bQF],
                            lambda v, qc=qc, q=q, sl=sl: v.tensor_tensor(
                                out=QFB[:, qc * TH + q * TQ:qc * TH + (q + 1) * TQ],
                                in0=tqf[:, sl], in1=teq[:, sl], op=ALU.add))
            for qc in range(DC):
                self.dma("sync", [bQF], [self.bD["scr_qf"]], batch=("qfs", h),
                         fn=lambda s, qc=qc, h=h: s.dma_start(
                             out=dd["scr_qf"][qc * NP:(qc + 1) * NP, h * TH:(h + 1) * TH],
                             in_=QFB[:, qc * TH:(qc + 1) * TH]))
        self.A.mark()
        btmp2 = Buf("abu_tmp")
        tmp2u = self.A.f32(DC * TQ, btmp2)
        self.emit_apply_rows(0, 0, T, self.xr, self.bxr,
                             self.xr, self.bxr, tmp2u, btmp2,
                             batched=(self.xr3, self.xr3))
        self.A.release()
        self.A.release()
        self.A.mark()
        bKF, bVB = Buf("kfb"), Buf("vb")
        bTKF, bTKE = Buf("tkf"), Buf("tke")
        bQKVBB = Buf("qkvbb")
        c_qkvbb = self.A.bf16(3 * self.D, bQKVBB)
        self.dma("gpsimd", [self.bD["qkv_b"]], [bQKVBB],
                 fn=lambda g: g.dma_start(out=c_qkvbb[0:1, :], in_=dd["qkv_b"]))
        bKVWA = [Buf(f"kvwa{c}") for c in range(DC)]
        kvw_all = self.A.bf16(DC * 2 * D, bKVWA)
        for cin in range(DC):
            self.dma("gpsimd", [self.bD["qkv_w"]], [bKVWA[cin]], batch="kvw",
                     fn=lambda g, cin=cin: g.dma_start(
                         out=kvw_all[:, cin * 2 * D:(cin + 1) * 2 * D],
                         in_=dd["qkv_w"][cin * NP:(cin + 1) * NP, D:3 * D]))
        KFB = self.A.bf16(self.TTH * D, bKF)
        VB = self.A.bf16(self.TTH * NH * (self.HD + 1), bVB)
        VROW = NH * (self.HD + 1)
        tkf = self.A.f32(TQ, bTKF)
        tke = self.A.f32(TQ, bTKE)
        NKV = 2 * D // TQ
        kvacc = [4, 5, 6]
        bKVA = [self.bBank[4], self.bBank[5], self.bBank[6]]
        PACK = 7
        for h in range(2):
            t0h = h * TH
            self.op("vector", [bVB] if h == 0 else [bVB], [bVB],
                    lambda v: v.memset(VB[:], 1.0)) if h == 0 else None
            for j in range(self.TTH):
                jt0 = t0h + j * NP
                for cin in range(DC):
                    def mmkv(t, cin=cin, jt0=jt0):
                        for sidx in range(NKV):
                            ii = t.matmul(out=self.pb[sidx][:, 0:TQ],
                                          lhsT=self.xnb(cin, jt0, jt0 + NP),
                                          rhs=kvw_all[:, cin * 2 * D + sidx * TQ:
                                                      cin * 2 * D + (sidx + 1) * TQ],
                                          start=(cin == 0), stop=False)
                        return ii
                    self.op("tensor", [bKVWA[cin]] + self.bxn(cin, jt0, jt0 + NP),
                            [self.bBank[s] for s in range(NKV)], mmkv)

                def mmbias(t, jt0=jt0):
                    for sidx in range(NKV):
                        ii = t.matmul(out=self.pb[sidx][:, 0:TQ], lhsT=self.c_ones_colb[0:1, :],
                                      rhs=c_qkvbb[0:1, D + sidx * TQ:D + (sidx + 1) * TQ],
                                      start=False, stop=True)
                    return ii
                self.op("tensor", [self.bConst, bQKVBB], [self.bBank[s] for s in range(NKV)], mmbias)
                nks = D // TQ
                for sidx in range(NKV):
                    if sidx < nks:
                        sl = slice(0, TQ)
                        self.op("vector", [self.bBank[sidx]], [bTKF],
                                lambda v, sidx=sidx, sl=sl: v.tensor_scalar_min(
                                    out=tkf[:, sl], in0=self.pb[sidx][:, 0:TQ], scalar1=0.0))
                        self.op("scalar", [bTKF], [bTKE],
                                lambda a, sl=sl: a.activation(out=tke[:, sl], in_=tkf[:, sl],
                                                              func=AF.Exp))
                        self.op("vector", [self.bBank[sidx], bTKE], [bKF],
                                lambda v, sidx=sidx, j=j, sl=sl: v.scalar_tensor_tensor(
                                    out=KFB[:, j * D + sidx * TQ:j * D + (sidx + 1) * TQ],
                                    in0=self.pb[sidx][:, 0:TQ], scalar=0.0,
                                    in1=tke[:, sl], op0=ALU.max, op1=ALU.add))
                    else:
                        v0 = (sidx - nks) * TQ
                        hpt = TQ // self.HD
                        h0 = v0 // self.HD

                        def evv(a, sidx=sidx, j=j, h0=h0, hpt=hpt):
                            src = self.pb[sidx][:, 0:TQ].rearrange("p (hh d) -> p hh d", d=self.HD)
                            dst = VB[:, j * VROW + h0 * (self.HD + 1):
                                     j * VROW + (h0 + hpt) * (self.HD + 1)]
                            dst = dst.rearrange("p (hh d) -> p hh d", d=self.HD + 1)[:, :, 0:self.HD]
                            return a.copy(out=dst, in_=src)
                        self.op("scalar", [self.bBank[sidx]], [bVB], evv)
                for hd in range(NH):
                    kb = kvacc[hd // PACK]
                    col = (hd % PACK) * (self.HD + 1)
                    first_in_bank = (hd % PACK == 0)
                    last_in_bank = (hd % PACK == PACK - 1) or (hd == NH - 1)
                    self.op("tensor", [bKF, bVB],
                            [bKVA[hd // PACK]],
                            lambda t, hd=hd, j=j, kb=kb, col=col, h=h,
                            fb=first_in_bank, lb=last_in_bank: t.matmul(
                                out=self.pb[kb][0:self.HD, col:col + self.HD + 1],
                                lhsT=KFB[:, j * D + hd * self.HD:j * D + (hd + 1) * self.HD],
                                rhs=VB[:, j * VROW + hd * (self.HD + 1):
                                       j * VROW + (hd + 1) * (self.HD + 1)],
                                start=(h == 0 and j == 0 and fb),
                                stop=(h == 1 and j == self.TTH - 1 and lb)))
        bKVB = Buf("kvb")
        KVB = self.A.bf16(NH * (self.HD + 1), bKVB)
        for hd in range(NH):
            kb = kvacc[hd // PACK]
            col = (hd % PACK) * (self.HD + 1)
            self.op("scalar", [bKVA[hd // PACK]], [bKVB],
                    lambda a, hd=hd, kb=kb, col=col: a.copy(
                        out=KVB[0:self.HD, hd * (self.HD + 1):(hd + 1) * (self.HD + 1)],
                        in_=self.pb[kb][0:self.HD, col:col + self.HD + 1]))
        self.A.release()
        self.A.mark()
        bQHA = Buf("qha")
        bYB = Buf("yb")
        bRD = [Buf(f"rden{p}") for p in range(3)]
        bRD2 = [Buf(f"rden2{p}") for p in range(3)]
        bRB = [Buf("rbc0"), Buf("rbc1")]
        bOW = [Buf(f"ow{c}") for c in range(DC)]
        QHA = self.A.bf16(NH * TH, bQHA)
        YB = self.A.bf16(DC * TH, bYB)
        rden = self.A.f32(TQ, bRD)
        rden2 = self.A.f32(TQ, bRD2)
        rbc = self.A.f32(2 * TQ, bRB)
        ow = self.A.bf16(DC * D, bOW)
        HD = self.HD
        for cin in range(DC):
            self.dma("gpsimd", [self.bD["out_w"]], [bOW[cin]], batch="ow",
                     fn=lambda g, cin=cin: g.dma_start(
                         out=ow[:, cin * D:(cin + 1) * D],
                         in_=dd["out_w"][cin * NP:(cin + 1) * NP, :]))
        for h in range(2):
            t0h = h * TH
            for hd in range(NH):
                self.dma("sync", [self.bD["scr_qf"]], [bQHA], batch=("qha", h),
                         fn=lambda s, hd=hd, h=h: s.dma_start(
                             out=QHA[0:HD, hd * TH:(hd + 1) * TH],
                             in_=dd["scr_qf"][hd * HD:(hd + 1) * HD, h * TH:(h + 1) * TH]))
            tasks = [(hd, q) for hd in range(NH) for q in range(self.TCH)]

            def mmy_k(k):
                hd, q = tasks[k]
                rb0 = (hd * HD) % NP
                q0 = q * TQ
                kb, kb2, p = k % 4, 4 + k % 3, k % 3

                def mmy(t, hd=hd, rb0=rb0, q0=q0, kb=kb, kb2=kb2, p=p):
                    t.matmul(
                        out=self.pb[kb][rb0:rb0 + HD, 0:TQ],
                        lhsT=KVB[0:HD, hd * (HD + 1):hd * (HD + 1) + HD],
                        rhs=QHA[0:HD, hd * TH + q0:hd * TH + q0 + TQ], start=True, stop=True)
                    return t.matmul(
                        out=self.pb[kb2][32 * p:32 * p + 1, 0:TQ],
                        lhsT=KVB[0:HD, hd * (HD + 1) + HD:(hd + 1) * (HD + 1)],
                        rhs=QHA[0:HD, hd * TH + q0:hd * TH + q0 + TQ], start=True, stop=True)
                self.op("tensor", [bKVB, bQHA], [self.bBank[kb], self.bBank[kb2]], mmy)

            n_tasks = len(tasks)

            def s_maxlnexp(k):
                p, kb2 = k % 3, 4 + k % 3
                self.op("vector", [self.bBank[kb2]], [bRD[p]],
                        lambda v, kb2=kb2, p=p: v.tensor_scalar_max(
                            out=rden[32 * p:32 * p + 1, 0:TQ],
                            in0=self.pb[kb2][32 * p:32 * p + 1, 0:TQ],
                            scalar1=1e-6))
                self.op("scalar", [bRD[p]], [bRD2[p]],
                        lambda a, p=p: a.activation(
                            out=rden2[32 * p:32 * p + 1, 0:TQ],
                            in_=rden[32 * p:32 * p + 1, 0:TQ], func=AF.Ln))
                self.op("scalar", [bRD2[p], self.bConst], [bRD[p]],
                        lambda a, p=p: a.activation(
                            out=rden[32 * p:32 * p + 1, 0:TQ],
                            in_=rden2[32 * p:32 * p + 1, 0:TQ],
                            func=AF.Exp, scale=self.ck(-1.0, 32 * p, 1)))

            def s_tail(k):
                hd, q = tasks[k]
                rb0 = (hd * HD) % NP
                q0 = q * TQ
                kb, kb2, p, pc = k % 4, 4 + k % 3, k % 3, k % 2
                self.op("tensor", [bRD[p], self.bConst], [self.bBank[kb2]],
                        lambda t, kb2=kb2, rb0=rb0, p=p: t.matmul(
                            out=self.pb[kb2][rb0:rb0 + HD, 0:TQ],
                            lhsT=self.c_ones_col[32 * p:32 * p + 1, 0:HD],
                            rhs=rden[32 * p:32 * p + 1, 0:TQ], start=True, stop=True))
                self.op("scalar", [self.bBank[kb2]], [bRB[pc]],
                        lambda a, kb2=kb2, rb0=rb0, pc=pc: a.copy(
                            out=rbc[rb0:rb0 + HD, pc * TQ:pc * TQ + TQ],
                            in_=self.pb[kb2][rb0:rb0 + HD, 0:TQ]))
                self.op("vector", [self.bBank[kb], bRB[pc]], [bYB],
                        lambda v, hd=hd, rb0=rb0, q0=q0, kb=kb, pc=pc: v.tensor_tensor(
                            out=YB[rb0:rb0 + HD, ((hd * HD) // NP) * TH + q0:
                                   ((hd * HD) // NP) * TH + q0 + TQ],
                            in0=self.pb[kb][rb0:rb0 + HD, 0:TQ],
                            in1=rbc[rb0:rb0 + HD, pc * TQ:pc * TQ + TQ], op=ALU.mult))

            mmy_k(0)
            mmy_k(1)
            s_maxlnexp(0)
            for i in range(n_tasks):
                if i + 2 < n_tasks:
                    mmy_k(i + 2)
                if i + 1 < n_tasks:
                    s_maxlnexp(i + 1)
                s_tail(i)
            for cout in range(DC):
                for q in range(self.TCH):
                    q0 = t0h + q * TQ
                    kb = 4 + (q % 2)

                    def mmo(t, cout=cout, q0=q0, kb=kb, t0h=t0h):
                        for cin in range(DC):
                            ii = t.matmul(out=self.pb[kb][:, 0:TQ],
                                          lhsT=ow[:, cin * D + cout * NP:cin * D + (cout + 1) * NP],
                                          rhs=YB[:, cin * TH + q0 - t0h:cin * TH + q0 - t0h + TQ],
                                          start=(cin == 0), stop=(cin == DC - 1))
                        return ii
                    self.op("tensor", list(bOW) + [bYB], [self.bBank[kb]], mmo)
                    self.op("vector", [self.bBank[kb], self.bConst] + self.bxr(cout, q0, q0 + TQ),
                            self.bxr(cout, q0, q0 + TQ),
                            lambda v, cout=cout, q0=q0, kb=kb: v.scalar_tensor_tensor(
                                out=self.xr(cout, q0, q0 + TQ), in0=self.pb[kb][:, 0:TQ],
                                scalar=self.c_outb[:, cout:cout + 1],
                                in1=self.xr(cout, q0, q0 + TQ), op0=ALU.add, op1=ALU.add))
        self.A.release()

        self.A.mark()
        btmp = Buf("ab2_tmp")
        tmp = self.A.f32(DC * TQ, btmp)
        for h in range(2):
            self.emit_ln_rows(h * TH, (h + 1) * TH)
            self.emit_apply_rows(0, h * TH, (h + 1) * TH, self.xr, self.bxr,
                                 self.xnb, lambda c, a, b: self.bxn(c, a, b), tmp, btmp,
                                 gcol=self.c_g2, becol=self.c_be2,
                                 batched=(self.xr3, self.xnb3))
        NFQ = 8 if FC % 8 == 0 else 4
        FQC = FC // NFQ
        bW1 = [[Buf(f"w1q{p}_{c}") for c in range(DC)] for p in range(2)]
        bW2 = [[Buf(f"w2q{p}_{c}") for c in range(FC // NFQ)] for p in range(2)]
        bHQ = Buf("hq")
        w1q = [self.A.bf16(DC * FQC * NP, bW1[0]), self.A.bf16(DC * FQC * NP, bW1[1])]
        w2q = [self.A.bf16(FQC * D, bW2[0]), self.A.bf16(FQC * D, bW2[1])]
        hq = self.A.bf16(FQC * TQ, bHQ)
        FQW = FQC * NP
        for i in range(self.TC):
            t0, t1 = i * TQ, (i + 1) * TQ
            for fq in range(NFQ):
                pp = fq % 2
                for cin in range(DC):
                    self.dma("gpsimd", [self.bD["w1"]], [bW1[pp][cin]], batch=("w1", i, fq),
                             fn=lambda g, cin=cin, fq=fq, pp=pp: g.dma_start(
                                 out=w1q[pp][:, cin * FQW:(cin + 1) * FQW],
                                 in_=dd["w1"][cin * NP:(cin + 1) * NP,
                                              fq * FQW:(fq + 1) * FQW]))
                for fc in range(FQC):
                    self.dma("gpsimd", [self.bD["w2"]], [bW2[pp][fc]], batch=("w2", i, fq),
                             fn=lambda g, fc=fc, fq=fq, pp=pp: g.dma_start(
                                 out=w2q[pp][:, fc * D:(fc + 1) * D],
                                 in_=dd["w2"][(fq * FQC + fc) * NP:(fq * FQC + fc + 1) * NP, :]))
                for fc in range(FQC):
                    kb = fc % 2

                    def mm1(t, fc=fc, t0=t0, t1=t1, kb=kb, pp=pp):
                        for cin in range(DC):
                            ii = t.matmul(out=self.pb[kb][:, 0:TQ],
                                          lhsT=w1q[pp][:, cin * FQW + fc * NP:
                                                       cin * FQW + (fc + 1) * NP],
                                          rhs=self.xnb(cin, t0, t1),
                                          start=(cin == 0), stop=(cin == DC - 1))
                        return ii
                    self.op("tensor", list(bW1[pp]) + [b for cc in range(DC) for b in self.bxn(cc, t0, t1)],
                            [self.bBank[kb]], mm1)
                    self.op("scalar", [self.bBank[kb], self.bConst], [bHQ],
                            lambda a, fc=fc, fq=fq, kb=kb: a.activation(
                                out=hq[:, fc * TQ:(fc + 1) * TQ], in_=self.pb[kb][:, 0:TQ],
                                func=AF.Gelu,
                                bias=self.c_b1[:, fq * FQC + fc:fq * FQC + fc + 1]))
                for dc_ in range(DC):
                    kd = 4 + (dc_ % 2)

                    def mm2(t, dc_=dc_, kd=kd, pp=pp):
                        for fc in range(FQC):
                            ii = t.matmul(out=self.pb[kd][:, 0:TQ],
                                          lhsT=w2q[pp][:, fc * D + dc_ * NP:fc * D + (dc_ + 1) * NP],
                                          rhs=hq[:, fc * TQ:(fc + 1) * TQ],
                                          start=(fc == 0), stop=(fc == FQC - 1))
                        return ii
                    self.op("tensor", list(bW2[pp]) + [bHQ], [self.bBank[kd]], mm2)
                    if fq == 0:
                        self.op("vector", [self.bBank[kd], self.bConst] + self.bxr(dc_, t0, t1),
                                self.bxr(dc_, t0, t1),
                                lambda v, dc_=dc_, t0=t0, t1=t1, kd=kd: v.scalar_tensor_tensor(
                                    out=self.xr(dc_, t0, t1), in0=self.pb[kd][:, 0:TQ],
                                    scalar=self.c_b2[:, dc_:dc_ + 1],
                                    in1=self.xr(dc_, t0, t1), op0=ALU.add, op1=ALU.add))
                    else:
                        self.op("vector", [self.bBank[kd]] + self.bxr(dc_, t0, t1),
                                self.bxr(dc_, t0, t1),
                                lambda v, dc_=dc_, t0=t0, t1=t1, kd=kd: v.tensor_tensor(
                                    out=self.xr(dc_, t0, t1), in0=self.xr(dc_, t0, t1),
                                    in1=self.pb[kd][:, 0:TQ], op=ALU.add))
        self.A.release()
        self.A.mark()
        bXL = [Buf("xl0"), Buf("xl1")]
        xl = [self.A.f32(TH, bXL[0]), self.A.f32(TH, bXL[1])]
        for c in range(DC):
            for h in range(2):
                st = (c * 2 + h) % 2
                self.dma("sync", [self.bD["scr_x"], bXL[st]], [bXL[st]],
                         lambda s, c=c, h=h, st=st: s.dma_start(
                             out=xl[st][:], in_=dd["scr_x"][c * NP:(c + 1) * NP,
                                                            h * TH:(h + 1) * TH]))
                self.op("vector", [bXL[st]] + [self.bXR[c][h]], [self.bXR[c][h]],
                        lambda v, c=c, h=h, st=st: v.tensor_tensor(
                            out=self.xr(c, h * TH, (h + 1) * TH),
                            in0=self.xr(c, h * TH, (h + 1) * TH),
                            in1=xl[st][:], op=ALU.add))
        self.A.release()

    def phase_store(self):
        dd, DC, TH = self.dd, self.DC, self.TH
        nst = 0
        for c in range(DC):
            for h in range(2):
                self.dma("sync", [self.bXR[c][h]], [self.bD["outT"]],
                         lambda s, c=c, h=h: s.dma_start(
                             out=self.out_d[c * NP:(c + 1) * NP, h * TH:(h + 1) * TH],
                             in_=self.xr(c, h * TH, (h + 1) * TH)))
                nst += 1
        target = self.cLD.n

        def fin(s, cache):
            s.wait_ge(self.cLD.h, target)
        self.pend["sync"].append(fin)


def _prep_inputs(cfg, ins):
    D, T, NST, R, E = cfg["D"], cfg["T"], cfg["NST"], cfg["R"], cfg["E"]
    I, FF = cfg["I"], cfg["FF"]
    DC = D // NP
    t = lambda a: np.ascontiguousarray(np.asarray(a).T)
    r = lambda a, k: np.ascontiguousarray(np.asarray(a).reshape(k, NP).T)

    def r2(a, k0, k1):
        a = np.asarray(a).reshape(k0, NP, k1)
        return np.ascontiguousarray(a.transpose(1, 0, 2).reshape(NP, k0 * k1))

    out = {}
    out["w_inT"] = t(ins["L0_attn_a_W_in"])
    out["conv_w_r"] = r2(ins["L0_attn_a_conv_w"], DC, 3)
    wx = t(ins["L0_attn_a_W_x"])
    wxp = np.zeros((D, NP), np.float32)
    wxp[:, 0:NST] = wx[:, R:R + NST]
    wxp[:, 32:32 + NST] = wx[:, R + NST:R + 2 * NST]
    wxp[:, 64:64 + R] = wx[:, 0:R]
    out["w_xT"] = wxp
    out["w_dtT"] = t(ins["L0_attn_a_W_dt"])
    out["b_dt_r"] = r(ins["L0_attn_a_b_dt"], DC)
    out["na_r"] = r2(ins["L0_attn_a_A_log"], DC, NST)
    out["d_param_r"] = r(ins["L0_attn_a_D_param"], DC)
    out["w_cT"] = t(ins["L0_attn_a_W_C"])
    out["w_outT"] = t(ins["L0_attn_a_W_out"])
    out["qkv_w"] = np.ascontiguousarray(ins["L0_attn_b_qkv_w"])
    out["qkv_b"] = np.asarray(ins["L0_attn_b_qkv_b"]).reshape(1, -1)
    out["qkv_b_r"] = r(ins["L0_attn_b_qkv_b"], 3 * D // NP)
    out["out_w"] = np.ascontiguousarray(ins["L0_attn_b_out_w"])
    out["out_b_r"] = r(ins["L0_attn_b_out_b"], DC)
    out["w1"] = np.ascontiguousarray(ins["L0_attn_b_w1"])
    out["b1_r"] = r(ins["L0_attn_b_b1"], FF // NP)
    out["w2"] = np.ascontiguousarray(ins["L0_attn_b_w2"])
    out["b2_r"] = r(ins["L0_attn_b_b2"], DC)
    out["g1_r"] = r(ins["L0_attn_b_g1"], DC)
    out["be1_r"] = r(ins["L0_attn_b_be1"], DC)
    out["g2_r"] = r(ins["L0_attn_b_g2"], DC)
    out["be2_r"] = r(ins["L0_attn_b_be2"], DC)
    out["wg_sT"] = t(ins["L0_ffn_Wg_s"])
    out["wu_sT"] = t(ins["L0_ffn_Wu_s"])
    out["wd_sT"] = t(ins["L0_ffn_Wd_s"])
    out["wgT"] = np.ascontiguousarray(np.asarray(ins["L0_ffn_Wg"]).transpose(0, 2, 1))
    out["wuT"] = np.ascontiguousarray(np.asarray(ins["L0_ffn_Wu"]).transpose(0, 2, 1))
    out["wdT"] = np.ascontiguousarray(np.asarray(ins["L0_ffn_Wd"]).transpose(0, 2, 1))
    out["wrT"] = t(ins["L0_ffn_Wr"])
    out["rb"] = np.asarray(ins["L0_ffn_rb"]).reshape(1, -1)
    out["ident"] = np.eye(NP, dtype=np.float32)
    e2 = np.zeros((2, 2 * NP), np.float32)
    e2[0, 0:NP] = 1.0
    e2[1, NP:2 * NP] = 1.0
    out["e2"] = e2
    out["ones_col"] = np.ones((1, NP), np.float32)
    out["ones_k"] = np.ones((NP, 1), np.float32)
    e8 = np.zeros((E, E * NP), np.float32)
    for n in range(E):
        e8[n, n * NP:(n + 1) * NP] = 1.0
    out["e8"] = e8
    for k in out:
        out[k] = np.ascontiguousarray(np.asarray(out[k], dtype=np.float32))
    return out


def kernel(**inputs):
    cfg = default_cfg()
    mk = MK(cfg)
    nc = mk.build()
    shared = _prep_inputs(cfg, inputs)
    x = np.asarray(inputs["L0_attn_a_x"], np.float32)
    B = x.shape[0]
    in_maps = []
    for b in range(B):
        m = dict(shared)
        m["xT"] = np.ascontiguousarray(x[b].T)
        in_maps.append(m)
    res = run_bass_kernel_spmd(nc, in_maps, core_ids=list(range(B)))
    outs = [np.ascontiguousarray(r["outT"].T) for r in res.results]
    return np.stack(outs, axis=0)
